# version 23
# baseline (speedup 1.0000x reference)
import numpy as np
import ml_dtypes
from contextlib import ExitStack

import concourse.bass as bass
import concourse.mybir as mybir
from concourse.bass_utils import run_bass_kernel_spmd

F32 = mybir.dt.float32
BF16 = mybir.dt.bfloat16
I32 = mybir.dt.int32
AF = mybir.ActivationFunctionType
ALU = mybir.AluOpType
AX = mybir.AxisListType

NEG = -3.0e38
INV_SQRT_HD = 2.0 / np.sqrt(128.0)
UC = 40.0


class CFG:
    def __init__(self, D=2048, T=2048, HID=5632, TtF=1024, TtK=512, QG=256):
        self.D, self.T, self.HID = D, T, HID
        self.NDC = D // 128
        self.NH = D // 128
        self.NHC = HID // 128
        self.TtF, self.TtK, self.QG = TtF, TtK, QG
        self.theta = 10000.0
        self.eps = 1e-5


FULL = CFG(TtK=512)


def slices(width, step=512):
    out = []
    o = 0
    while o < width:
        w = min(step, width - o)
        out.append((o, w))
        o += w
    return out



class Op:
    __slots__ = ("idx", "stream", "kind", "fn", "deps", "inc_sem", "val", "waits", "tag")

    def __init__(self, idx, stream, kind, fn, tag):
        self.idx, self.stream, self.kind, self.fn, self.tag = idx, stream, kind, fn, tag
        self.deps = set()
        self.inc_sem = None
        self.val = None
        self.waits = []


class Prog:
    CSTREAMS = ("pe", "act", "dve", "pool")

    NLANES = {"qs": 8, "qg": 6, "qa": 4}

    def __init__(self):
        self.ops = []
        self.last_w = {}
        self.readers = {}
        self.last_real = {}
        self.dcnt = {"qs": 0, "qg": 0, "qa": 0}
        self.lane_prev = {}

    def add(self, stream, fn, reads=(), writes=(), kind="c", tag=""):
        idx = len(self.ops)
        op = Op(idx, stream, kind, fn, tag)
        for k in reads:
            d = self.last_w.get(k)
            if d is not None:
                op.deps.add(d)
        for k in writes:
            d = self.last_w.get(k)
            if d is not None:
                op.deps.add(d)
            for r in self.readers.get(k, ()):
                op.deps.add(r)
        op.deps = {d for d in op.deps
                   if not (stream == "pe" and self.ops[d].stream == "pe")}
        for k in writes:
            self.last_w[k] = idx
            self.readers[k] = []
        for k in reads:
            self.readers.setdefault(k, []).append(idx)
        self.ops.append(op)
        self.last_real[stream] = idx
        return idx

    def dma(self, stream, fn, reads=(), writes=(), tag=""):
        queue = {"pool": "qg", "act": "qa"}.get(stream, "qs")
        j = self.dcnt[queue]
        self.dcnt[queue] = j + 1
        nl = self.NLANES[queue]
        lane = f"{queue}{j % nl}"
        idx = self.add(stream, fn, reads, writes, kind="d", tag=tag)
        op = self.ops[idx]
        op.inc_sem = lane
        op.val = 16 * (j // nl + 1)
        prev = self.lane_prev.get(lane)
        if prev is not None:
            op.deps.add(prev)
        self.lane_prev[lane] = idx
        return idx

    def barrier(self, tag="bar"):
        dep_set = set(self.last_real.values()) | set(self.lane_prev.values())
        for s in ("pe", "act", "dve", "pool", "qs"):
            idx = len(self.ops)
            op = Op(idx, s, "w", None, tag)
            op.deps = set(dep_set)
            self.ops.append(op)

    def finalize(self, block, sems):
        need_inc = set()
        for op in self.ops:
            for d in op.deps:
                need_inc.add(d)
        cnt = {k: 0 for k in sems}
        for op in self.ops:
            if op.kind == "c" and op.idx in need_inc:
                sem = op.stream
                cnt[sem] += 1
                op.inc_sem, op.val = sem, cnt[sem]
        observed = {s: {} for s in ("pe", "act", "dve", "pool", "qs")}
        snap = {}
        for op in self.ops:
            obs = observed[op.stream]
            for d in sorted(op.deps):
                dop = self.ops[d]
                if dop.inc_sem is None:
                    continue
                if obs.get(dop.inc_sem, 0) >= dop.val:
                    continue
                op.waits.append((dop.inc_sem, dop.val))
                dsnap = snap.get(d)
                if dsnap:
                    for k, v in dsnap.items():
                        if obs.get(k, 0) < v:
                            obs[k] = v
                if obs.get(dop.inc_sem, 0) < dop.val:
                    obs[dop.inc_sem] = dop.val
            if op.inc_sem is not None:
                snap[op.idx] = dict(obs)
        per = {s: [] for s in ("pe", "act", "dve", "pool", "qs")}
        for op in self.ops:
            per[op.stream].append(op)
        stats = {s: (len(per[s]), sum(len(o.waits) for o in per[s])) for s in per}

        def replay(stream):
            def go(eng):
                for op in per[stream]:
                    for sem, val in op.waits:
                        eng.wait_ge(sems[sem], val)
                    if op.fn is not None:
                        inst = op.fn(eng)
                        if op.inc_sem is not None:
                            inst.then_inc(sems[op.inc_sem], 16 if op.kind == "d" else 1)
            return go

        block.tensor(replay("pe"))
        block.scalar(replay("act"))
        block.vector(replay("dve"))
        block.gpsimd(replay("pool"))
        block.sync(replay("qs"))
        return stats



def rkeys(name, dc, t0, t1):
    return [f"{name}:{dc}:{b}" for b in range(t0 // 256, (t1 + 255) // 256)]


def pbanks(name, off, width):
    return [f"{name}:b{b}" for b in range(off // 512, (off + width + 511) // 512)]


class Ctx:
    pass


def build_nc(cfg, phases=("ffn1", "attn", "ffn2"), ffn1_convert=True):
    nc = bass.Bass("TRN2", target_bir_lowering=False)
    c = Ctx()
    c.cfg = cfg
    c.nc = nc
    P = Prog()
    c.P = P
    D, T, HID = cfg.D, cfg.T, cfg.HID
    NDC, NH, NHC = cfg.NDC, cfg.NH, cfg.NHC

    dt = lambda n, s, ty, k="ExternalInput": nc.dram_tensor(n, s, ty, kind=k).ap()
    c.xT = dt("xT", [D, T], F32)
    c.wq_t = dt("wq_t", [NH * 128, NDC * 128], F32)
    c.wk_t = dt("wk_t", [NH * 128, NDC * 128], F32)
    c.wv_t = dt("wv_t", [NH * 128, NDC * 128], F32)
    c.wo_t = dt("wo_t", [NDC * 128, NH * 128], F32)
    c.bq_col = dt("bq_col", [128, NH], F32)
    c.bk_col = dt("bk_col", [128, NH], F32)
    c.bv_col = dt("bv_col", [128, NH], F32)
    c.bo_col = dt("bo_col", [128, NDC], F32)
    c.gam_col = dt("gam_col", [128, NDC], F32)
    c.wg_t = dt("wg_t", [NHC * 128, NDC * 128], I32)
    c.wu_t = dt("wu_t", [NHC * 128, NDC * 128], I32)
    c.wd_t = dt("wd_t", [NDC * 128, NHC * 128], I32)
    c.sg_col = dt("sg_col", [128, NHC], F32)
    c.su_col = dt("su_col", [128, NHC], F32)
    c.sd_col = dt("sd_col", [128, NDC], F32)
    c.ce_t = dt("ce_t", [128, T], BF16)
    c.se_t = dt("se_t", [128, T], BF16)
    c.maskw_s = dt("maskw_s", [128, 640], BF16)
    c.mask_c1 = dt("mask_c1", [128, cfg.QG], BF16)
    c.mask_c2 = dt("mask_c2", [128, cfg.QG], BF16)
    c.ident_f = dt("ident_f", [128, 128], F32)
    c.ident_bf = dt("ident_bf", [128, 128], BF16)
    c.R1 = dt("R1", [D, T], F32, "Internal")
    c.R2 = dt("R2", [D, T], F32, "Internal")
    c.wg_sc = dt("wg_sc", [NHC * 128, NDC * 128], BF16, "Internal")
    c.wu_sc = dt("wu_sc", [NHC * 128, NDC * 128], BF16, "Internal")
    c.wd_sc = dt("wd_sc", [NDC * 128, NHC * 128], BF16, "Internal")
    c.wq_sc = dt("wq_sc", [NH * 128, NDC * 128], BF16, "Internal")
    c.wk_sc = dt("wk_sc", [NH * 128, NDC * 128], BF16, "Internal")
    c.wv_sc = dt("wv_sc", [NH * 128, NDC * 128], BF16, "Internal")
    c.wo_sc = dt("wo_sc", [NDC * 128, NH * 128], BF16, "Internal")
    c.out = dt("out", [D, T], F32, "ExternalOutput")

    es = ExitStack()
    with es:
        sb = lambda n, s, ty: es.enter_context(nc.sbuf_tensor(n, s, ty))
        c.ones_col_bf = sb("ones_col_bf", [128, 1], BF16)
        c.ones_row_f = sb("ones_row_f", [33, 128], F32)
        c.ones_row_bf = sb("ones_row_bf", [1, 128], BF16)
        c.s_ident_f = sb("s_ident_f", [128, 128], F32)
        c.s_ident_bf = sb("s_ident_bf", [128, 128], BF16)
        c.s_maskw = sb("s_maskw", [128, 640], BF16)
        c.s_mc1 = sb("s_mc1", [128, cfg.QG], BF16)
        c.s_mc2 = sb("s_mc2", [128, cfg.QG], BF16)
        c.s_bq = sb("s_bq", [128, NH], F32)
        c.s_bk = sb("s_bk", [128, NH], F32)
        c.s_bv = sb("s_bv", [128, NH], F32)
        c.s_bo = sb("s_bo", [128, NDC], F32)
        c.s_gam = sb("s_gam", [128, NDC], F32)
        c.s_sg = sb("s_sg", [128, NHC], F32)
        c.s_su = sb("s_su", [128, NHC], F32)
        c.s_sde = sb("s_sde", [128, NDC], F32)

        sems = {}
        sem_names = ["pe", "act", "dve", "pool"]
        sem_names += [f"qs{i}" for i in range(Prog.NLANES["qs"])]
        sem_names += [f"qg{i}" for i in range(Prog.NLANES["qg"])]
        sem_names += [f"qa{i}" for i in range(Prog.NLANES["qa"])]
        for name in sem_names:
            sems[name] = es.enter_context(nc.semaphore("sem_" + name))

        emit_consts(c)
        if "ffn1" in phases:
            with ExitStack() as ph:
                F = alloc_ffn(c, ph, es)
                emit_ffn(c, F, src=c.xT, srcn="xT", dst=c.R1, dstn="R1", convert=ffn1_convert)
            P.barrier("p1end")
        if "attn" in phases:
            with ExitStack() as ph:
                alloc_attn(c, ph)
                emit_group, finish_groups = make_attn_groups(c)
                gpt = max(cfg.TtK // cfg.QG, 1)

                def after_tile(kt):
                    for g in range(kt * gpt, (kt + 1) * gpt):
                        if g < cfg.T // cfg.QG:
                            emit_group(g)

                emit_kv_prep(c, after_tile=after_tile)
                finish_groups()
            P.barrier("p2end")
        if "ffn2" in phases:
            with ExitStack() as ph:
                F = alloc_ffn(c, ph, es)
                emit_ffn(c, F, src=c.R2, srcn="R2", dst=c.out, dstn="out", convert=False)
        P.barrier("final")

        block = es.enter_context(nc.Block())
        c.stats = P.finalize(block, sems)
    return nc, c


def emit_consts(c):
    P, nc, cfg = c.P, c.nc, c.cfg
    loads = [
        (c.s_bq, c.bq_col, "s_bq"), (c.s_bk, c.bk_col, "s_bk"),
        (c.s_bv, c.bv_col, "s_bv"), (c.s_bo, c.bo_col, "s_bo"),
        (c.s_gam, c.gam_col, "s_gam"), (c.s_sg, c.sg_col, "s_sg"),
        (c.s_su, c.su_col, "s_su"),
        (c.s_ident_f, c.ident_f, "s_ident_f"), (c.s_ident_bf, c.ident_bf, "s_ident_bf"),
        (c.s_maskw, c.maskw_s, "s_maskw"), (c.s_mc1, c.mask_c1, "s_mc1"),
        (c.s_mc2, c.mask_c2, "s_mc2"),
    ]
    for dstt, src, key in loads:
        P.dma("qs", lambda e, d=dstt, s=src: e.dma_start(d[:], s), writes=[key])
    P.dma("qs", lambda e: e.dma_start(c.s_sde[:], c.sd_col), writes=["s_sde"])
    P.add("dve", lambda e: e.tensor_scalar(c.s_sde[:], c.s_sde[:], 0.5, None, ALU.mult),
          reads=["s_sde"], writes=["s_sde"])
    P.add("pool", lambda e: e.memset(c.ones_col_bf[:], 1.0), writes=["ones_col_bf"])
    P.add("pool", lambda e: e.memset(c.ones_row_f[:], 1.0), writes=["ones_row_f"])
    P.add("pool", lambda e: e.memset(c.ones_row_bf[:], 1.0), writes=["ones_row_bf"])


def alloc_ffn(c, ph, es):
    from types import SimpleNamespace
    nc, cfg = c.nc, c.cfg
    sb = lambda n, s, ty: ph.enter_context(nc.sbuf_tensor(n, s, ty))
    pt = lambda n, s, ty: ph.enter_context(nc.psum_tensor(n, s, ty))
    NDC, NHC, TtF = cfg.NDC, cfg.NHC, cfg.TtF
    F = SimpleNamespace()
    u = alloc_ffn.count = getattr(alloc_ffn, "count", 0) + 1
    F.rbf = [sb(f"f{u}_rbf{i}", [128, TtF], BF16) for i in range(NDC)]
    F.g = [sb(f"f{u}_g{i}", [128, TtF], BF16) for i in range(NHC)]
    F.rsq = [sb(f"f{u}_rsq{i}", [128, TtF], BF16) for i in range(2)]
    F.gt = [sb(f"f{u}_gt{i}", [128, TtF], F32) for i in range(2)]
    F.rwb = [sb(f"f{u}_rwb{i}", [1, TtF], BF16) for i in range(2)]
    F.wg = [sb(f"f{u}_wg{i}", [128, NDC * 128], BF16) for i in range(2)]
    F.wu = [sb(f"f{u}_wu{i}", [128, NDC * 128], BF16) for i in range(2)]
    F.wd = [sb(f"f{u}_wd{i}", [128, NHC * 128], BF16) for i in range(2)]
    F.rc2 = [sb(f"f{u}_rc2{i}", [128, TtF], F32) for i in range(2)]
    F.Br = sb(f"f{u}_Br", [128, TtF], BF16)
    F.Bm = sb(f"f{u}_Bm", [128, TtF], BF16)
    F.rows = [sb(f"f{u}_row{i}", [1, TtF], F32) for i in range(3)]
    F.pp = [pt(f"f{u}_pp{i}", [128, TtF], F32) for i in range(4)]
    F.cvt = [sb(f"f{u}_cvt{i}", [128, (NDC * 128) // 2], BF16) for i in range(1)]
    return F


def conv_tasks_ffn(c):
    return None


def emit_conv_attn(c, F, tasks, n):
    P, cfg = c.P, c.cfg
    for _ in range(n):
        if not tasks:
            return
        src, dst, row, key = tasks.pop(0)
        buf = F.cvt[0]
        bk = "cvtbuf0"
        W = src.tensor.shape[1]
        hw = W // 2
        for pc in range(2):
            P.dma("pool", lambda e, b=buf, s=src, r=row, o=pc * hw, hw=hw: e.dma_start(
                b[:, 0:hw], s[r:r + 128, o:o + hw]),
                reads=[], writes=[bk], tag="cvA")
            P.dma("act", lambda e, b=buf, d=dst, r=row, o=pc * hw, hw=hw: e.dma_start(
                d[r:r + 128, o:o + hw], b[:, 0:hw]),
                reads=[bk], writes=[key], tag="cvAw")


def emit_ffn(c, F, src, srcn, dst, dstn, convert):
    P, nc, cfg = c.P, c.nc, c.cfg
    D, T, HID, TtF = cfg.D, cfg.T, cfg.HID, cfg.TtF
    NDC, NHC = cfg.NDC, cfg.NHC
    nsl = slices(TtF)

    atasks = []
    if convert:
        for oc in range(cfg.NH):
            atasks.append((c.wk_t, c.wk_sc, oc * 128, f"wk:{oc}"))
            atasks.append((c.wv_t, c.wv_sc, oc * 128, f"wv:{oc}"))
        for oc in range(cfg.NH):
            atasks.append((c.wq_t, c.wq_sc, oc * 128, f"wq:{oc}"))
        for doc in range(NDC):
            atasks.append((c.wo_t, c.wo_sc, doc * 128, f"wo:{doc}"))

    for ti in range(T // TtF):
        t0 = ti * TtF
        pp = F.pp
        for dc in range(NDC):
            r = dc % 2
            P.dma("pool", lambda e, dc=dc, t0=t0: e.dma_start(
                F.rbf[dc][:], src[dc * 128:(dc + 1) * 128, t0:t0 + TtF]),
                reads=rkeys(srcn, dc, t0, t0 + TtF), writes=[f"frbf{dc}"], tag="lnR")
            P.add("dve", lambda e, dc=dc, r=r: e.tensor_mul(
                F.rsq[r][:], F.rbf[dc][:], F.rbf[dc][:]),
                reads=[f"frbf{dc}"], writes=[f"frsq{r}"])
            for (o, w) in nsl:
                P.add("pe", lambda e, dc=dc, o=o, w=w: e.matmul(
                    pp[0][0:1, o:o + w], c.ones_col_bf[:], F.rbf[dc][:, o:o + w],
                    start=(dc == 0), stop=(dc == NDC - 1)),
                    reads=["ones_col_bf", f"frbf{dc}"], writes=pbanks("fpp0", o, w))
                P.add("pe", lambda e, dc=dc, r=r, o=o, w=w: e.matmul(
                    pp[1][0:1, o:o + w], c.ones_col_bf[:], F.rsq[r][:, o:o + w],
                    start=(dc == 0), stop=(dc == NDC - 1)),
                    reads=["ones_col_bf", f"frsq{r}"], writes=pbanks("fpp1", o, w))
        rw = F.rows
        allp0 = pbanks("fpp0", 0, TtF)
        allp1 = pbanks("fpp1", 0, TtF)
        P.add("dve", lambda e: e.tensor_scalar(rw[0][:], pp[0][0:1, :], 1.0 / D, None, ALU.mult),
              reads=allp0, writes=["frow0"])
        P.add("dve", lambda e: e.tensor_scalar(rw[1][:], pp[1][0:1, :], 1.0 / D, cfg.eps,
                                               ALU.mult, ALU.add),
              reads=allp1, writes=["frow1"])
        P.add("dve", lambda e: e.tensor_mul(rw[2][:], rw[0][:], rw[0][:]),
              reads=["frow0"], writes=["frow2"])
        P.add("dve", lambda e: e.tensor_sub(rw[1][:], rw[1][:], rw[2][:]),
              reads=["frow1", "frow2"], writes=["frow1"])
        P.add("act", lambda e: e.activation(rw[2][:], rw[1][:], AF.Ln),
              reads=["frow1"], writes=["frow2"])
        P.add("act", lambda e: e.activation(rw[1][:], rw[2][:], AF.Exp, scale=-0.5),
              reads=["frow2"], writes=["frow1"])
        P.add("dve", lambda e: e.tensor_copy(F.rwb[0][:], rw[1][:]),
              reads=["frow1"], writes=["frwb0"])
        P.add("dve", lambda e: e.tensor_mul(rw[0][:], rw[0][:], rw[1][:]),
              reads=["frow0", "frow1"], writes=["frow0"])
        P.add("dve", lambda e: e.tensor_copy(F.rwb[1][:], rw[0][:]),
              reads=["frow0"], writes=["frwb1"])
        for (o, w) in nsl:
            P.add("pe", lambda e, o=o, w=w: e.matmul(
                pp[2][:, o:o + w], c.ones_row_bf[:], F.rwb[0][0:1, o:o + w],
                start=True, stop=True),
                reads=["ones_row_bf", "frwb0"], writes=pbanks("fpp2", o, w))
            P.add("pe", lambda e, o=o, w=w: e.matmul(
                pp[3][:, o:o + w], c.ones_row_bf[:], F.rwb[1][0:1, o:o + w],
                start=True, stop=True),
                reads=["ones_row_bf", "frwb1"], writes=pbanks("fpp3", o, w))
        P.add("act", lambda e: e.activation(F.Br[:], pp[2][:], AF.Copy),
              reads=pbanks("fpp2", 0, TtF), writes=["fBr"])
        P.add("act", lambda e: e.activation(F.Bm[:], pp[3][:], AF.Copy),
              reads=pbanks("fpp3", 0, TtF), writes=["fBm"])
        for dc in range(NDC):
            P.add("dve", lambda e, dc=dc: e.tensor_mul(
                F.rbf[dc][:], F.rbf[dc][:], F.Br[:]),
                reads=[f"frbf{dc}", "fBr"], writes=[f"frbf{dc}"])
            P.add("dve", lambda e, dc=dc: e.tensor_sub(
                F.rbf[dc][:], F.rbf[dc][:], F.Bm[:]),
                reads=[f"frbf{dc}", "fBm"], writes=[f"frbf{dc}"])

        hcols = NDC * 64
        for hc in range(NHC):
            par = hc % 2
            gw, uw = F.wg[par], F.wu[par]
            if convert and ti == 0:
                for hx in range(2):
                    P.dma("pool", lambda e, hc=hc, b=gw, o=hx * hcols: e.dma_start(
                        b[:, o:o + hcols],
                        c.wg_t[hc * 128:(hc + 1) * 128, o:o + hcols]),
                        writes=[f"fwg{par}h{hx}"], tag="cvG")
                    P.add("dve", lambda e, b=gw, o=hx * hcols: e.tensor_scalar(
                        b[:, o:o + hcols], b[:, o:o + hcols], -128.0, None, ALU.add),
                        reads=[f"fwg{par}h{hx}"], writes=[f"fwg{par}h{hx}"])
                    P.dma("act", lambda e, hc=hc, b=gw, o=hx * hcols: e.dma_start(
                        c.wg_sc[hc * 128:(hc + 1) * 128, o:o + hcols],
                        b[:, o:o + hcols]),
                        reads=[f"fwg{par}h{hx}"], writes=[f"wg:{hc}"])
                for hx in range(2):
                    P.dma("pool", lambda e, hc=hc, b=uw, o=hx * hcols: e.dma_start(
                        b[:, o:o + hcols],
                        c.wu_t[hc * 128:(hc + 1) * 128, o:o + hcols]),
                        writes=[f"fwu{par}h{hx}"], tag="cvU")
                    P.add("dve", lambda e, b=uw, o=hx * hcols: e.tensor_scalar(
                        b[:, o:o + hcols], b[:, o:o + hcols], -128.0, None, ALU.add),
                        reads=[f"fwu{par}h{hx}"], writes=[f"fwu{par}h{hx}"])
                    P.dma("act", lambda e, hc=hc, b=uw, o=hx * hcols: e.dma_start(
                        c.wu_sc[hc * 128:(hc + 1) * 128, o:o + hcols],
                        b[:, o:o + hcols]),
                        reads=[f"fwu{par}h{hx}"], writes=[f"wu:{hc}"])
            else:
                P.dma("qs", lambda e, hc=hc, b=gw: e.dma_start(
                    b[:], c.wg_sc[hc * 128:(hc + 1) * 128, :]),
                    reads=[f"wg:{hc}"], writes=[f"fwg{par}h0", f"fwg{par}h1"])
                P.dma("qs", lambda e, hc=hc, b=uw: e.dma_start(
                    b[:], c.wu_sc[hc * 128:(hc + 1) * 128, :]),
                    reads=[f"wu:{hc}"], writes=[f"fwu{par}h0", f"fwu{par}h1"])
                if convert:
                    emit_conv_attn(c, F, atasks, 1)
            ppg, ppu = pp[par], pp[2 + par]
            gk, uk = f"fpp{par}", f"fpp{2+par}"
            for dc in range(NDC):
                hx = dc // (NDC // 2)
                for (o, w) in nsl:
                    P.add("pe", lambda e, dc=dc, o=o, w=w, ppg=ppg, gw=gw, hx=hx:
                          e.matmul(
                        ppg[:, o:o + w], gw[:, dc * 128:(dc + 1) * 128],
                        F.rbf[dc][:, o:o + w],
                        start=(dc == 0), stop=(dc == NDC - 1)),
                        reads=[f"fwg{par}h{hx}", f"frbf{dc}"], writes=pbanks(gk, o, w))
            for dc in range(NDC):
                hx = dc // (NDC // 2)
                for (o, w) in nsl:
                    P.add("pe", lambda e, dc=dc, o=o, w=w, ppu=ppu, uw=uw, hx=hx:
                          e.matmul(
                        ppu[:, o:o + w], uw[:, dc * 128:(dc + 1) * 128],
                        F.rbf[dc][:, o:o + w],
                        start=(dc == 0), stop=(dc == NDC - 1)),
                        reads=[f"fwu{par}h{hx}", f"frbf{dc}"], writes=pbanks(uk, o, w))
            gt = F.gt[par]
            P.add("act", lambda e, hc=hc, ppg=ppg, gt=gt: e.activation(
                gt[:], ppg[:], AF.Silu, scale=c.s_sg[:, hc:hc + 1]),
                reads=pbanks(gk, 0, TtF) + ["s_sg"], writes=[f"fgt{par}"])
            P.add("dve", lambda e, hc=hc, ppu=ppu, gt=gt: e.scalar_tensor_tensor(
                F.g[hc][:], ppu[:], c.s_su[:, hc:hc + 1], gt[:], ALU.mult, ALU.mult),
                reads=pbanks(uk, 0, TtF) + ["s_su", f"fgt{par}"], writes=[f"fg{hc}"])

        for doc in range(NDC):
            par = doc % 2
            dw = F.wd[par]
            qcols = (NHC * 128) // 4
            if convert and ti == 0:
                for qx in range(4):
                    P.dma("pool", lambda e, doc=doc, b=dw, o=qx * qcols: e.dma_start(
                        b[:, o:o + qcols],
                        c.wd_t[doc * 128:(doc + 1) * 128, o:o + qcols]),
                        writes=[f"fwd{par}q{qx}"], tag="cvD")
                    P.add("dve", lambda e, b=dw, o=qx * qcols: e.tensor_scalar(
                        b[:, o:o + qcols], b[:, o:o + qcols], -128.0, None, ALU.add),
                        reads=[f"fwd{par}q{qx}"], writes=[f"fwd{par}q{qx}"])
                    P.dma("act", lambda e, doc=doc, b=dw, o=qx * qcols: e.dma_start(
                        c.wd_sc[doc * 128:(doc + 1) * 128, o:o + qcols],
                        b[:, o:o + qcols]),
                        reads=[f"fwd{par}q{qx}"], writes=[f"wd:{doc}"])
            else:
                P.dma("qs", lambda e, doc=doc, b=dw: e.dma_start(
                    b[:], c.wd_sc[doc * 128:(doc + 1) * 128, :]),
                    reads=[f"wd:{doc}"],
                    writes=[f"fwd{par}q{qx}" for qx in range(4)])
                if convert:
                    emit_conv_attn(c, F, atasks, 2)
            P.dma("qs", lambda e, doc=doc, par=par, t0=t0: e.dma_start(
                F.rc2[par][:], src[doc * 128:(doc + 1) * 128, t0:t0 + TtF]),
                reads=rkeys(srcn, doc, t0, t0 + TtF), writes=[f"frc2{par}"])
            ppd = pp[par]
            dk = f"fpp{par}"
            for hc in range(NHC):
                qx = hc // (NHC // 4)
                for (o, w) in nsl:
                    P.add("pe", lambda e, hc=hc, o=o, w=w, ppd=ppd, dw=dw, qx=qx:
                          e.matmul(
                        ppd[:, o:o + w], dw[:, hc * 128:(hc + 1) * 128],
                        F.g[hc][:, o:o + w],
                        start=(hc == 0), stop=(hc == NHC - 1)),
                        reads=[f"fwd{par}q{qx}", f"fg{hc}"], writes=pbanks(dk, o, w))
            rc2 = F.rc2[par]
            P.add("dve", lambda e, doc=doc, ppd=ppd, rc2=rc2, par=par: e.scalar_tensor_tensor(
                rc2[:], ppd[:], c.s_sde[:, doc:doc + 1], rc2[:], ALU.mult, ALU.add),
                reads=pbanks(dk, 0, TtF) + ["s_sde", f"frc2{par}"], writes=[f"frc2{par}"])
            P.dma("act", lambda e, doc=doc, rc2=rc2, t0=t0: e.dma_start(
                dst[doc * 128:(doc + 1) * 128, t0:t0 + TtF], rc2[:]),
                reads=[f"frc2{par}"], writes=rkeys(dstn, doc, t0, t0 + TtF))
    if convert:
        emit_conv_attn(c, F, atasks, len(atasks))


def alloc_attn(c, ph):
    nc, cfg = c.nc, c.cfg
    sb = lambda n, s, ty: ph.enter_context(nc.sbuf_tensor(n, s, ty))
    pt = lambda n, s, ty: ph.enter_context(nc.psum_tensor(n, s, ty))
    D, T, NDC, NH = cfg.D, cfg.T, cfg.NDC, cfg.NH
    TtK, QG = cfg.TtK, cfg.QG
    TN = min(256, TtK)
    c.a_ce = [sb(f"a_ce{i}", [128, TtK], BF16) for i in range(2)]
    c.a_se = [sb(f"a_se{i}", [128, TtK], BF16) for i in range(2)]
    c.a_K = [sb(f"a_K{i}", [128, T], BF16) for i in range((NH + 1) // 2)]
    c.a_V = sb("a_V", [128, (T // 128) * D], BF16)
    c.a_xk = [sb(f"a_xk{i}", [128, TtK], BF16) for i in range(NDC)]
    c.a_rst = [sb(f"a_rst{i}", [128, TN], BF16) for i in range(4)]
    c.a_rsq = [sb(f"a_rsq{i}", [128, TN], BF16) for i in range(2)]
    c.a_Bs = [sb(f"a_Bs{i}", [128, TN], BF16) for i in range(2)]
    c.a_Bm = [sb(f"a_Bm{i}", [128, TN], BF16) for i in range(2)]
    c.a_rowFt = [sb(f"a_rowFt{k}", [64, 2 * TN], F32) for k in range(2)]
    c.a_rowF = [[c.a_rowFt[k][(s % 2) * 32:(s % 2) * 32 + 1,
                              (s // 2) * TN:(s // 2) * TN + TN]
                 for s in range(4)] for k in range(2)]
    c.a_BmF = [sb(f"a_BmF{i}", [128, QG], F32) for i in range(2)]
    c.a_Br1F = [sb(f"a_Br1F{i}", [128, QG], F32) for i in range(2)]
    c.a_rows = [sb(f"a_row{i}", [1, TN], F32) for i in range(6)]
    c.a_rwb = [sb(f"a_rwb{i}", [1, TN], BF16) for i in range(2)]
    c.a_z = [sb(f"a_z{i}", [128, QG], F32) for i in range(2)]
    c.a_kf = [sb(f"a_kf{i}", [128, TtK], BF16) for i in range(4)]
    c.a_rt = [sb(f"a_rt{i}", [128, TtK], BF16) for i in range(2)]
    c.a_wA = [sb(f"a_wA{i}", [128, NDC * 128], BF16) for i in range(2)]
    c.a_wB = [sb(f"a_wB{i}", [128, max(NDC, NH) * 128], BF16) for i in range(2)]
    c.a_vt = [sb(f"a_vt{i}", [128, TtK], BF16) for i in range(2)]
    c.a_rq = [sb(f"a_rq{i}", [128, QG], F32) for i in range(2)]
    c.a_q = [sb(f"a_q{i}", [128, QG], BF16) for i in range((NH + 1) // 2)]
    c.a_qf = [sb(f"a_qf{i}", [128, QG], BF16) for i in range(4)]
    c.a_qrt = [sb(f"a_qrt{i}", [128, QG], BF16) for i in range(2)]
    c.a_ksq = [sb("a_ksq0", [128, TtK], BF16)]
    c.a_mk2 = sb("a_mk2", [1, max(NH, 4)], F32)
    c.a_qm2 = sb("a_qm2", [1, max(NH, 4)], F32)
    c.a_ubr = [sb(f"a_ubr{i}", [1, max(NH, 4)], F32) for i in range(2)]
    c.a_ubias = [sb(f"a_ubias{i}", [128, 1], F32) for i in range(2)]
    c.a_us = sb("a_us", [1, 4], F32)
    c.a_ceq = sb("a_ceq", [128, QG], BF16)
    c.a_seq = sb("a_seq", [128, QG], BF16)
    c.a_PT = [sb(f"a_PT{i}", [128, (T // 128) * QG], BF16) for i in range(2)]
    c.a_linv = sb("a_linv", [1, QG], F32)
    c.a_linv_bf = [sb(f"a_linv_bf{i}", [1, QG], BF16) for i in range(2)]
    c.a_Blinv = sb("a_Blinv", [128, QG], BF16)
    c.a_ao = sb("a_ao", [128, NH * QG], BF16)
    c.a_oc = [sb(f"a_oc{i}", [128, QG], F32) for i in range(2)]
    c.a_pA = pt("a_pA", [128, 1024], F32)
    c.a_pB = pt("a_pB", [128, 1024], F32)
    c.a_pPV = pt("a_pPV", [128, 256], F32)
    c.a_pL = pt("a_pL", [1, 512], F32)
    c.a_pC = pt("a_pC", [128, 512], F32)
    c.a_pT = pt("a_pT", [128, 512], BF16)


def emit_rope(c, srcA, srcAk, srcB, srcBk, dst_tensor, dstks, t0, W, qmode,
              ceb, cek, seb, sek):
    P = c.P
    rt = c.a_qrt if qmode else c.a_rt
    rk = "aqrt" if qmode else "art"
    P.add("dve", lambda e: e.tensor_mul(rt[0][:, 0:W], ceb[:, 0:W], srcA[:, 0:W]),
          reads=[cek, srcAk], writes=[rk + "0"])
    P.add("dve", lambda e: e.tensor_mul(rt[1][:, 0:W], seb[:, 0:W], srcB[:, 0:W]),
          reads=[sek, srcBk], writes=[rk + "1"])
    P.add("dve", lambda e: e.tensor_add(dst_tensor[:, :] if qmode else
                                        dst_tensor[:, t0:t0 + W],
                                        rt[0][:, 0:W], rt[1][:, 0:W]),
          reads=[rk + "0", rk + "1"], writes=dstks)


def emit_norm_rows(c, N):
    P, cfg = c.P, c.cfg
    D, eps = cfg.D, cfg.eps
    rw = c.a_rows
    A = lambda i: rw[i][0:1, 0:N]
    P.add("dve", lambda e: e.tensor_scalar(A(0), c.a_pL[0:1, 0:N], 1.0 / D, None, ALU.mult),
          reads=["apL"], writes=["arow0"])
    P.add("dve", lambda e: e.tensor_scalar(A(1), c.a_pC[0:1, 0:N], 1.0 / D, eps,
                                           ALU.mult, ALU.add),
          reads=["apC"], writes=["arow1"])
    P.add("dve", lambda e: e.tensor_mul(A(2), A(0), A(0)), reads=["arow0"], writes=["arow2"])
    P.add("dve", lambda e: e.tensor_sub(A(1), A(1), A(2)),
          reads=["arow1", "arow2"], writes=["arow1"])
    P.add("act", lambda e: e.activation(A(2), A(1), AF.Ln), reads=["arow1"], writes=["arow2"])
    P.add("act", lambda e: e.activation(A(3), A(2), AF.Exp, scale=-0.5),
          reads=["arow2"], writes=["arow3"])
    P.add("dve", lambda e: e.tensor_scalar(A(4), A(1), -eps, None, ALU.add),
          reads=["arow1"], writes=["arow4"])
    P.add("dve", lambda e: e.reciprocal(A(5), A(1)), reads=["arow1"], writes=["arow5"])
    P.add("dve", lambda e: e.tensor_mul(A(4), A(4), A(5)),
          reads=["arow4", "arow5"], writes=["arow4"])
    P.add("dve", lambda e: e.tensor_scalar(A(4), A(4), cfg.eps, None, ALU.add),
          reads=["arow4"], writes=["arow4"])
    P.add("act", lambda e: e.activation(A(5), A(4), AF.Ln), reads=["arow4"], writes=["arow5"])
    P.add("act", lambda e: e.activation(A(5), A(5), AF.Exp, scale=-0.5),
          reads=["arow5"], writes=["arow5"])
    P.add("dve", lambda e: e.tensor_mul(A(4), A(3), A(5)),
          reads=["arow3", "arow5"], writes=["arow4"])
    P.add("dve", lambda e: e.tensor_copy(c.a_rwb[0][0:1, 0:N], A(0)),
          reads=["arow0"], writes=["arwb0"])
    P.add("dve", lambda e: e.tensor_copy(c.a_rwb[1][0:1, 0:N], A(4)),
          reads=["arow4"], writes=["arwb1"])


def emit_norm_stats(c, srcname, bufs, bufkeyf, t0, N):
    P, cfg = c.P, c.cfg
    NDC = cfg.NDC
    nsl = slices(N)
    for dc in range(NDC):
        r = dc % 2
        b = bufs[dc % len(bufs)]
        bk = bufkeyf(dc)
        P.dma("pool", lambda e, dc=dc, b=b, N=N, t0=t0: e.dma_start(
            b[:, 0:N], c.R1[dc * 128:(dc + 1) * 128, t0:t0 + N]),
            reads=rkeys("R1", dc, t0, t0 + N), writes=[bk])
        P.add("dve", lambda e, b=b, r=r, N=N: e.tensor_mul(
            c.a_rsq[r][:, 0:N], b[:, 0:N], b[:, 0:N]),
            reads=[bk], writes=[f"arsq{r}"])
        for (o, w) in nsl:
            P.add("pe", lambda e, b=b, o=o, w=w, dc=dc: e.matmul(
                c.a_pL[0:1, o:o + w], c.ones_col_bf[:], b[:, o:o + w],
                start=(dc == 0), stop=(dc == NDC - 1)),
                reads=["ones_col_bf", bk], writes=["apL"])
            P.add("pe", lambda e, r=r, o=o, w=w, dc=dc: e.matmul(
                c.a_pC[0:1, o:o + w], c.ones_col_bf[:], c.a_rsq[r][:, o:o + w],
                start=(dc == 0), stop=(dc == NDC - 1)),
                reads=["ones_col_bf", f"arsq{r}"], writes=["apC"])


def emit_bcast(c, rowap, rowkey, dstap, dstkey, N, f32=False, o0=0, pb=0):
    P = c.P
    ones = c.ones_row_f[pb:pb + 1, :] if f32 else c.ones_row_bf[:]
    okey = "ones_row_f" if f32 else "ones_row_bf"
    for (o, w) in slices(N):
        P.add("pe", lambda e, o=o, w=w: e.matmul(
            c.a_pC[:, o:o + w], ones, rowap[0:1, o0 + o:o0 + o + w],
            start=True, stop=True),
            reads=[okey, rowkey], writes=["apC"])
    P.add("act", lambda e: e.activation(dstap[:, 0:N], c.a_pC[:, 0:N], AF.Copy),
          reads=["apC"], writes=[dstkey])


def emit_kv_prep(c, after_tile=None):
    P, cfg = c.P, c.cfg
    D, T, NDC, NH, TtK = c.cfg.D, c.cfg.T, c.cfg.NDC, c.cfg.NH, c.cfg.TtK
    nkt = T // TtK
    TN = min(256, TtK)
    nsub = TtK // TN
    nhalf = nkt * nsub

    def stats(j):
        emit_norm_stats(c, "R1", c.a_rst, lambda dc: f"arst{dc%4}", j * TN, TN)

    def rowsbc(j):
        jp = j % 2
        js = j % 4
        emit_norm_rows(c, TN)
        emit_bcast(c, c.a_rwb[0], "arwb0", c.a_Bm[jp], f"aBm{jp}", TN)
        emit_bcast(c, c.a_rwb[1], "arwb1", c.a_Bs[jp], f"aBs{jp}", TN)
        P.add("dve", lambda e, js=js: e.tensor_copy(
            c.a_rowF[0][js][0:1, 0:TN], c.a_rows[0][0:1, 0:TN]),
            reads=["arow0"], writes=[f"arF0_{js}"])
        P.add("dve", lambda e, js=js: e.tensor_copy(
            c.a_rowF[1][js][0:1, 0:TN], c.a_rows[3][0:1, 0:TN]),
            reads=["arow3"], writes=[f"arF1_{js}"])

    def xn(j):
        t0 = j * TN
        so = (j % nsub) * TN
        jp = j % 2
        Bm, Bs = c.a_Bm[jp], c.a_Bs[jp]
        for dc in range(NDC):
            r = dc % 2
            P.dma("pool", lambda e, dc=dc, t0=t0, so=so: e.dma_start(
                c.a_xk[dc][:, so:so + TN], c.R1[dc * 128:(dc + 1) * 128, t0:t0 + TN]),
                reads=rkeys("R1", dc, t0, t0 + TN), writes=[f"axk{dc}"])
            P.add("dve", lambda e, dc=dc, r=r, so=so, Bm=Bm: e.tensor_sub(
                c.a_rsq[r][:, 0:TN], c.a_xk[dc][:, so:so + TN], Bm[:, 0:TN]),
                reads=[f"axk{dc}", f"aBm{jp}"], writes=[f"arsq{r}"])
            P.add("dve", lambda e, dc=dc, r=r, so=so, Bs=Bs: e.scalar_tensor_tensor(
                c.a_xk[dc][:, so:so + TN], c.a_rsq[r][:, 0:TN], c.s_gam[:, dc:dc + 1],
                Bs[:, 0:TN], ALU.mult, ALU.mult),
                reads=[f"arsq{r}", "s_gam", f"aBs{jp}"], writes=[f"axk{dc}"])

    def proj(kt):
        t0 = kt * TtK
        tpar = kt % 2
        ceb, seb = c.a_ce[tpar], c.a_se[tpar]
        cek, sek = f"a_ce{tpar}", f"a_se{tpar}"
        P.dma("qs", lambda e, t0=t0, ceb=ceb: e.dma_start(
            ceb[:, 0:TtK], c.ce_t[:, t0:t0 + TtK]), writes=[cek])
        P.dma("qs", lambda e, t0=t0, seb=seb: e.dma_start(
            seb[:, 0:TtK], c.se_t[:, t0:t0 + TtK]), writes=[sek])
        for i in range(NH // 2):
            kfs = []
            for AB in range(2):
                blk = AB * (NH // 2) + i
                par = (2 * i + AB) % 2
                wb = c.a_wA[par]
                for hf in range(2):
                    P.dma("qs", lambda e, blk=blk, wb=wb, hf=hf: e.dma_start(
                        wb[:, hf * NDC * 64:(hf + 1) * NDC * 64],
                        c.wk_sc[blk * 128:(blk + 1) * 128,
                                hf * NDC * 64:(hf + 1) * NDC * 64]),
                        reads=[f"wk:{blk}"], writes=[f"awA{par}h{hf}"])
                pk = c.a_pA[:, AB * 512: AB * 512 + TtK]
                pkk = pbanks("apA", AB * 512, TtK)
                for dc in range(NDC):
                    hf = dc // (NDC // 2)
                    P.add("pe", lambda e, dc=dc, wb=wb, pk=pk, par=par: e.matmul(
                        pk, wb[:, dc * 128:(dc + 1) * 128], c.a_xk[dc][:],
                        start=(dc == 0), stop=(dc == NDC - 1)),
                        reads=[f"awA{par}h{hf}", f"axk{dc}"], writes=pkk)
                kf = c.a_kf[(i % 2) * 2 + AB]
                kfk = f"akf{(i%2)*2+AB}"
                P.add("act", lambda e, blk=blk, pk=pk, kf=kf: e.activation(
                    kf[:], pk, AF.Identity, bias=c.s_bk[:, blk:blk + 1]),
                    reads=pkk + ["s_bk"], writes=[kfk])
                kfs.append((kf, kfk))
            emit_rope(c, kfs[0][0], kfs[0][1], kfs[1][0], kfs[1][1], c.a_K[i],
                      [f"K:{2*i}:{kt}", f"K:{2*i+1}:{kt}"], t0, TtK, False,
                      ceb, cek, seb, sek)
        for oc in range(NH):
            par = oc % 2
            wb = c.a_wB[par]
            for hf in range(2):
                P.dma("qs", lambda e, oc=oc, wb=wb, hf=hf: e.dma_start(
                    wb[:, hf * NDC * 64:(hf + 1) * NDC * 64],
                    c.wv_sc[oc * 128:(oc + 1) * 128,
                            hf * NDC * 64:(hf + 1) * NDC * 64]),
                    reads=[f"wv:{oc}"], writes=[f"awB{par}h{hf}"])
            pv = c.a_pB[:, par * 512: par * 512 + TtK]
            pvk = pbanks("apB", par * 512, TtK)
            for dc in range(NDC):
                hf = dc // (NDC // 2)
                P.add("pe", lambda e, dc=dc, oc=oc, wb=wb, pv=pv: e.matmul(
                    pv, wb[:, dc * 128:(dc + 1) * 128], c.a_xk[dc][:],
                    start=(dc == 0), stop=(dc == NDC - 1)),
                    reads=[f"awB{par}h{hf}", f"axk{dc}"], writes=pvk)
            vt = c.a_vt[par]
            P.add("act", lambda e, oc=oc, pv=pv, vt=vt: e.activation(
                vt[:, 0:TtK], pv, AF.Identity, bias=c.s_bv[:, oc:oc + 1]),
                reads=pvk + ["s_bv"], writes=[f"avt{par}"])
            for i in range(TtK // 128):
                P.add("pe", lambda e, i=i, vt=vt: e.transpose(
                    c.a_pT[:, i * 128:(i + 1) * 128], vt[:, i * 128:(i + 1) * 128],
                    c.s_ident_bf[:]),
                    reads=[f"avt{par}", "s_ident_bf"], writes=["apT"])
            for i in range(TtK // 128):
                tc = t0 // 128 + i
                P.add("act", lambda e, i=i, tc=tc, oc=oc: e.activation(
                    c.a_V[:, tc * c.cfg.D + oc * 128: tc * c.cfg.D + oc * 128 + 128],
                    c.a_pT[:, i * 128:(i + 1) * 128], AF.Copy),
                    reads=["apT"], writes=[f"V:{tc}:{oc}"])

    stats(0)
    rowsbc(0)
    if nhalf > 1:
        stats(1)
        rowsbc(1)
    nxt = min(nhalf, 2)
    for kt in range(nkt):
        for s in range(nsub):
            xn(kt * nsub + s)
        hi = min(nhalf, (kt + 1) * nsub + 2)
        pre = nxt < hi
        if pre:
            stats(nxt)
        proj(kt)
        if pre:
            rowsbc(nxt)
            nxt += 1
        while nxt < hi:
            stats(nxt)
            rowsbc(nxt)
            nxt += 1
        emit_k_norms(c, kt)
        if after_tile is not None:
            after_tile(kt)


def emit_k_norms(c, kt):
    P, cfg = c.P, c.cfg
    T, NH, TtK = cfg.T, cfg.NH, cfg.TtK
    t0 = kt * TtK
    for i in range(NH // 2):
        par = i % 2
        ks = c.a_ksq[par % len(c.a_ksq)]
        kkeys = [f"K:{2*i+j}:{kt}" for j in (0, 1)]
        P.add("act", lambda e, i=i, ks=ks, t0=t0: e.activation(
            ks[:, 0:TtK], c.a_K[i][:, t0:t0 + TtK], AF.Square),
            reads=kkeys, writes=["aksq0"])
        for hb2 in (0, 64):
            h = 2 * i + hb2 // 64
            for (o, w) in slices(TtK):
                first = kt == 0 and o == 0
                P.add("pe", lambda e, ks=ks, hb2=hb2, o=o, w=w: e.matmul(
                    c.a_pL[0:1, 0:w], c.ones_col_bf[hb2:hb2 + 64, 0:1],
                    ks[hb2:hb2 + 64, o:o + w], start=True, stop=True),
                    reads=["ones_col_bf", "aksq0"], writes=["apL"])
                P.add("dve", lambda e, w=w: e.tensor_reduce(
                    c.a_us[0:1, 0:1], c.a_pL[0:1, 0:w], AX.X, ALU.max),
                    reads=["apL"], writes=["aus"])
                if first:
                    P.add("dve", lambda e, h=h: e.tensor_copy(
                        c.a_mk2[0:1, h:h + 1], c.a_us[0:1, 0:1]),
                        reads=["aus"], writes=["amk2"])
                else:
                    P.add("dve", lambda e, h=h: e.tensor_max(
                        c.a_mk2[0:1, h:h + 1], c.a_mk2[0:1, h:h + 1],
                        c.a_us[0:1, 0:1]),
                        reads=["aus", "amk2"], writes=["amk2"])


def make_attn_groups(c):
    P, cfg = c.P, c.cfg
    D, T, NDC, NH, QG = cfg.D, cfg.T, cfg.NDC, cfg.NH, cfg.QG
    nq = QG // 128
    pA_names = [("apA", c.a_pA), ("apB", c.a_pB)]
    pending_outproj = [None]

    def outproj(g):
        qt0 = g * QG
        gp = g % 2
        for doc in range(NDC):
            par = doc % 2
            wb = c.a_wB[par]
            wcols = NH * 128
            for hf in range(2):
                P.dma("qs" if hf == 0 else "act",
                      lambda e, doc=doc, wb=wb, hf=hf, wcols=wcols:
                      e.dma_start(
                    wb[:, hf * wcols // 2:(hf + 1) * wcols // 2],
                    c.wo_sc[doc * 128:(doc + 1) * 128,
                            hf * wcols // 2:(hf + 1) * wcols // 2]),
                    reads=[f"wo:{doc}"], writes=[f"awB{par}h{hf}"])
            po = c.a_pB[:, par * 512: par * 512 + QG]
            pok = pbanks("apB", par * 512, QG)
            for oc in range(NH):
                hf = oc // (NH // 2)
                P.add("pe", lambda e, oc=oc, wb=wb, po=po: e.matmul(
                    po, wb[:, oc * 128:(oc + 1) * 128], c.a_ao[:, oc * QG:(oc + 1) * QG],
                    start=(oc == 0), stop=(oc == NH - 1)),
                    reads=[f"awB{par}h{hf}", f"ao:{oc}"], writes=pok)
            ocs = c.a_oc[par]
            zb = c.a_z[par]
            rqb = c.a_rq[par]
            P.dma("qs", lambda e, doc=doc, qt0=qt0, rqb=rqb: e.dma_start(
                rqb[:], c.R1[doc * 128:(doc + 1) * 128, qt0:qt0 + QG]),
                reads=rkeys("R1", doc, qt0, qt0 + QG), writes=[f"arq{par}"])
            P.add("dve", lambda e, zb=zb, rqb=rqb, gp=gp: e.tensor_sub(
                zb[:], rqb[:], c.a_BmF[gp][:, 0:QG]),
                reads=[f"arq{par}", f"aBmF{gp}"], writes=[f"az{par}"])
            P.add("dve", lambda e, zb=zb, gp=gp: e.tensor_mul(
                zb[:], zb[:], c.a_Br1F[gp][:, 0:QG]),
                reads=[f"az{par}", f"aBr1F{gp}"], writes=[f"az{par}"])
            P.add("dve", lambda e, doc=doc, po=po, ocs=ocs, rqb=rqb: e.scalar_tensor_tensor(
                ocs[:], po, c.s_bo[:, doc:doc + 1], rqb[:], ALU.add, ALU.add),
                reads=pok + ["s_bo", f"arq{par}"], writes=[f"aoc{par}"])
            P.add("dve", lambda e, ocs=ocs, zb=zb: e.tensor_add(ocs[:], ocs[:], zb[:]),
                  reads=[f"aoc{par}", f"az{par}"], writes=[f"aoc{par}"])
            P.dma("act", lambda e, doc=doc, ocs=ocs, qt0=qt0: e.dma_start(
                c.R2[doc * 128:(doc + 1) * 128, qt0:qt0 + QG], ocs[:]),
                reads=[f"aoc{par}"], writes=rkeys("R2", doc, qt0, qt0 + QG))

    def emit_one_group(g):
        qt0 = g * QG
        nt = (qt0 + QG) // 128
        gp = g % 2
        TN = min(256, cfg.TtK)
        js = (qt0 // TN) % 4
        off = qt0 % TN
        so = qt0 % cfg.TtK
        emit_bcast(c, c.a_rowF[0][js], f"arF0_{js}", c.a_BmF[gp], f"aBmF{gp}",
                   QG, f32=True, o0=off, pb=(js % 2) * 32)
        emit_bcast(c, c.a_rowF[1][js], f"arF1_{js}", c.a_Br1F[gp], f"aBr1F{gp}",
                   QG, f32=True, o0=off, pb=(js % 2) * 32)
        if pending_outproj[0] is not None:
            outproj(pending_outproj[0])
        pending_outproj[0] = g
        P.dma("qs", lambda e, qt0=qt0: e.dma_start(
            c.a_ceq[:, 0:QG], c.ce_t[:, qt0:qt0 + QG]), writes=["a_ceq"])
        P.dma("qs", lambda e, qt0=qt0: e.dma_start(
            c.a_seq[:, 0:QG], c.se_t[:, qt0:qt0 + QG]), writes=["a_seq"])
        for i in range(NH // 2):
            qfs = []
            for AB in range(2):
                blk = AB * (NH // 2) + i
                par = (2 * i + AB) % 2
                wb = c.a_wA[par]
                for hf in range(2):
                    P.dma("qs" if hf == 0 else "act",
                          lambda e, blk=blk, wb=wb, hf=hf: e.dma_start(
                        wb[:, hf * NDC * 64:(hf + 1) * NDC * 64],
                        c.wq_sc[blk * 128:(blk + 1) * 128,
                                hf * NDC * 64:(hf + 1) * NDC * 64]),
                        reads=[f"wq:{blk}"], writes=[f"awA{par}h{hf}"])
                pq = c.a_pA[:, AB * 512: AB * 512 + QG]
                pqk = pbanks("apA", AB * 512, QG)
                for dc in range(NDC):
                    hf = dc // (NDC // 2)
                    P.add("pe", lambda e, dc=dc, wb=wb, pq=pq, par=par, so=so:
                          e.matmul(
                        pq, wb[:, dc * 128:(dc + 1) * 128],
                        c.a_xk[dc][:, so:so + QG],
                        start=(dc == 0), stop=(dc == NDC - 1)),
                        reads=[f"awA{par}h{hf}", f"axk{dc}"], writes=pqk)
                qf = c.a_qf[(i % 2) * 2 + AB]
                qfk = f"aqf{(i%2)*2+AB}"
                P.add("act", lambda e, blk=blk, pq=pq, qf=qf: e.activation(
                    qf[:], pq, AF.Identity, bias=c.s_bq[:, blk:blk + 1]),
                    reads=pqk + ["s_bq"], writes=[qfk])
                qfs.append((qf, qfk))
            emit_rope(c, qfs[0][0], qfs[0][1], qfs[1][0], qfs[1][1], c.a_q[i],
                      [f"q:{2*i}", f"q:{2*i+1}"], qt0, QG, True,
                      c.a_ceq, "a_ceq", c.a_seq, "a_seq")
        for i in range(NH // 2):
            r = i % 2
            P.add("act", lambda e, i=i, r=r: e.activation(
                c.a_rsq[r][:, 0:QG], c.a_q[i][:, 0:QG], AF.Square),
                reads=[f"q:{2*i}", f"q:{2*i+1}"], writes=[f"arsq{r}"])
            for hb2 in (0, 64):
                h = 2 * i + hb2 // 64
                P.add("pe", lambda e, r=r, hb2=hb2: e.matmul(
                    c.a_pL[0:1, 0:QG], c.ones_col_bf[hb2:hb2 + 64, 0:1],
                    c.a_rsq[r][hb2:hb2 + 64, 0:QG], start=True, stop=True),
                    reads=["ones_col_bf", f"arsq{r}"], writes=["apL"])
                P.add("dve", lambda e, h=h: e.tensor_reduce(
                    c.a_qm2[0:1, h:h + 1], c.a_pL[0:1, 0:QG], AX.X, ALU.max),
                    reads=["apL"], writes=["aqm2"])
        ubr = c.a_ubr[g % 2]
        ubk = f"aubr{g%2}"
        P.add("dve", lambda e, ubr=ubr: e.tensor_mul(
            ubr[0:1, 0:NH], c.a_qm2[0:1, 0:NH], c.a_mk2[0:1, 0:NH]),
            reads=["aqm2", "amk2"], writes=[ubk])
        P.add("act", lambda e, ubr=ubr: e.activation(ubr[0:1, 0:NH], ubr[0:1, 0:NH],
                                                     AF.Ln),
              reads=[ubk], writes=[ubk])
        P.add("act", lambda e, ubr=ubr: e.activation(ubr[0:1, 0:NH], ubr[0:1, 0:NH],
                                                     AF.Exp, scale=0.5),
              reads=[ubk], writes=[ubk])
        P.add("dve", lambda e, ubr=ubr: e.tensor_scalar(
            ubr[0:1, 0:NH], ubr[0:1, 0:NH], -INV_SQRT_HD, UC, ALU.mult, ALU.add),
            reads=[ubk], writes=[ubk])
        rotbox = [0]

        def front(h):
            hb = (h % 2) * 64
            ub = c.a_ubias[h % 2]
            ubik = f"aubias{h%2}"
            P.add("pe", lambda e, h=h, ubr=ubr: e.matmul(
                c.a_pC[:, 0:1], c.ones_row_f[0:1, :], ubr[0:1, h:h + 1],
                start=True, stop=True),
                reads=["ones_row_f", ubk], writes=["apC"])
            P.add("act", lambda e, ub=ub: e.activation(ub[:, 0:1], c.a_pC[:, 0:1],
                                                       AF.Copy),
                  reads=["apC"], writes=[ubik])
            PT = c.a_PT[h % 2]
            ptk = f"PT{h%2}"
            nbat = QG * 4 // 256
            for tb0 in range(0, nt, nbat):
                tbw = min(nbat, nt - tb0)
                rot = rotbox[0]
                pname, pten = pA_names[rot % 2]
                rotbox[0] += 1
                for i in range(tbw):
                    tc = tb0 + i
                    qtc0 = qt0 // 128
                    if tc < qtc0:
                        mconst, mkey = None, None
                    elif tc == qtc0 or nq == 1:
                        mconst, mkey = c.s_mc1, "s_mc1"
                    else:
                        mconst, mkey = c.s_mc2, "s_mc2"
                    P.add("pe", lambda e, h=h, hb=hb, tc=tc, i=i, pten=pten,
                          st=(mconst is None): e.matmul(
                        pten[:, i * QG:(i + 1) * QG],
                        c.a_K[h // 2][hb:hb + 64, tc * 128:(tc + 1) * 128],
                        c.a_q[h // 2][hb:hb + 64, :],
                        start=True, stop=st),
                        reads=[f"K:{h}:{tc*128//cfg.TtK}", f"q:{h}"],
                        writes=pbanks(pname, i * QG, QG))
                    if mconst is not None:
                        P.add("pe", lambda e, i=i, mconst=mconst, pten=pten: e.matmul(
                            pten[:, i * QG:(i + 1) * QG],
                            c.s_ident_bf[:], mconst[:], start=False, stop=True),
                            reads=["s_ident_bf", mkey],
                            writes=pbanks(pname, i * QG, QG))
                P.add("act", lambda e, tb0=tb0, tbw=tbw, pten=pten, PT=PT,
                      ub=ub: e.activation(
                    PT[:, tb0 * QG:(tb0 + tbw) * QG], pten[:, 0:tbw * QG],
                    AF.Exp, scale=INV_SQRT_HD, bias=ub[:, 0:1]),
                    reads=pbanks(pname, 0, tbw * QG) + [ubik],
                    writes=[f"{ptk}:{tb0 + i}" for i in range(tbw)])
        def pv(h):
            PT = c.a_PT[h % 2]
            ptk = f"PT{h%2}"
            for tc in range(nt):
                P.add("pe", lambda e, h=h, tc=tc, PT=PT, nt=nt: e.matmul(
                    c.a_pPV[:, 0:QG],
                    c.a_V[:, tc * D + h * 128: tc * D + h * 128 + 128],
                    PT[:, tc * QG:(tc + 1) * QG],
                    start=(tc == 0), stop=(tc == nt - 1)),
                    reads=[f"V:{tc}:{h}", f"{ptk}:{tc}"], writes=["apPV"])
            for tc in range(nt):
                P.add("pe", lambda e, tc=tc, PT=PT, nt=nt: e.matmul(
                    c.a_pL[0:1, 0:QG], c.ones_col_bf[:], PT[:, tc * QG:(tc + 1) * QG],
                    start=(tc == 0), stop=(tc == nt - 1)),
                    reads=["ones_col_bf", f"{ptk}:{tc}"], writes=["apL"])
            P.add("act", lambda e, h=h: e.activation(
                c.a_ao[:, h * QG:(h + 1) * QG], c.a_pPV[:, 0:QG], AF.Copy),
                reads=["apPV"], writes=[f"ao:{h}"])
            P.add("dve", lambda e: e.reciprocal(c.a_linv[0:1, 0:QG], c.a_pL[0:1, 0:QG]),
                  reads=["apL"], writes=["alinv"])
            P.add("dve", lambda e, h=h: e.tensor_copy(
                c.a_linv_bf[h % 2][0:1, 0:QG], c.a_linv[0:1, 0:QG]),
                reads=["alinv"], writes=[f"alinv_bf{h%2}"])

        def fin(h):
            P.add("pe", lambda e, h=h: e.matmul(
                c.a_pC[:, 0:QG], c.ones_row_bf[:], c.a_linv_bf[h % 2][0:1, 0:QG],
                start=True, stop=True),
                reads=["ones_row_bf", f"alinv_bf{h%2}"], writes=["apC"])
            P.add("act", lambda e: e.activation(c.a_Blinv[:, 0:QG], c.a_pC[:, 0:QG],
                                                AF.Copy),
                  reads=["apC"], writes=["aBlinv"])
            P.add("dve", lambda e, h=h: e.tensor_mul(
                c.a_ao[:, h * QG:(h + 1) * QG], c.a_ao[:, h * QG:(h + 1) * QG],
                c.a_Blinv[:, 0:QG]),
                reads=[f"ao:{h}", "aBlinv"], writes=[f"ao:{h}"])

        for h in range(NH):
            front(h)
            if h >= 1:
                pv(h - 1)
            if h >= 2:
                fin(h - 2)
        pv(NH - 1)
        fin(NH - 2)
        fin(NH - 1)

    def finish():
        outproj(pending_outproj[0])

    return emit_one_group, finish



def prep_inputs(cfg, x_core, rms_w, wq, bq, wk, bk, wv, bv, wo, bo,
                wg, sg, wu, su, wd, sd):
    D, T, HID = cfg.D, cfg.T, cfg.HID
    NDC, NH, NHC = cfg.NDC, cfg.NH, cfg.NHC
    permA = np.concatenate([h * 128 + np.arange(0, 128, 2) for h in range(NH)])
    permB = np.concatenate([h * 128 + np.arange(1, 128, 2) for h in range(NH)])
    perm = np.concatenate([permA, permB])

    def tile_in_out(w):
        nin, nout = w.shape[0] // 128, w.shape[1] // 128
        return np.ascontiguousarray(
            w.reshape(nin, 128, nout, 128).transpose(2, 1, 0, 3).reshape(nout * 128, nin * 128))

    def tile_quant(w):
        nout, nin = w.shape[0] // 128, w.shape[1] // 128
        return np.ascontiguousarray(
            w.reshape(nout, 128, nin, 128).transpose(0, 3, 2, 1).reshape(nout * 128, nin * 128))

    col = lambda v: np.ascontiguousarray(v.reshape(-1, 128).T.astype(np.float32))

    j = np.arange(64)
    freqs = cfg.theta ** (j.astype(np.float64) * (-2.0 / 128.0))
    pos = np.arange(T, dtype=np.float64)
    ang = pos[None, :] * freqs[:, None]
    cosa, sina = np.cos(ang), np.sin(ang)
    even = (j % 2 == 0)[:, None]
    ce = np.where(even, cosa, sina).astype(np.float32)
    se = np.where(even, -sina, cosa).astype(np.float32)
    ce = np.concatenate([ce, ce], axis=0)
    se = np.concatenate([se, se], axis=0)

    ii = np.arange(128)
    mask_s = np.where(ii[None, :] <= ii[:, None], 0.0, NEG).astype(np.float32)
    mask_t = np.where(ii[:, None] <= ii[None, :], 0.0, NEG).astype(np.float32)
    mask_fl = np.full((128, 128), NEG, dtype=np.float32)
    zeros = np.zeros((128, 128), dtype=np.float32)
    maskw_s = np.concatenate([zeros] * 4 + [mask_s], axis=1)
    nq = cfg.QG // 128
    if nq == 1:
        mask_c1 = mask_t
        mask_c2 = mask_t
    else:
        mask_c1 = np.concatenate([mask_t, zeros], axis=1)
        mask_c2 = np.concatenate([mask_fl, mask_t], axis=1)
    ident = np.eye(128, dtype=np.float32)

    return {
        "xT": np.ascontiguousarray(x_core.T).astype(np.float32),
        "wq_t": tile_in_out(wq[:, perm]),
        "wk_t": tile_in_out(wk[:, perm]),
        "wv_t": tile_in_out(wv),
        "wo_t": tile_in_out(wo),
        "bq_col": col(bq[perm]), "bk_col": col(bk[perm]),
        "bv_col": col(bv), "bo_col": col(bo), "gam_col": col(rms_w),
        "wg_t": tile_quant(wg), "wu_t": tile_quant(wu), "wd_t": tile_quant(wd),
        "sg_col": col(sg), "su_col": col(su), "sd_col": col(sd),
        "ce_t": ce.astype(ml_dtypes.bfloat16), "se_t": se.astype(ml_dtypes.bfloat16),
        "maskw_s": maskw_s.astype(ml_dtypes.bfloat16),
        "mask_c1": mask_c1.astype(ml_dtypes.bfloat16),
        "mask_c2": mask_c2.astype(ml_dtypes.bfloat16),
        "ident_f": ident, "ident_bf": ident.astype(ml_dtypes.bfloat16),
    }


def kernel(L0_attn_x, L0_attn_rms_w, L0_attn_wq, L0_attn_bq, L0_attn_wk, L0_attn_bk,
           L0_attn_wv, L0_attn_bv, L0_attn_wo, L0_attn_bo,
           L0_ffn_wq_gate, L0_ffn_s_gate, L0_ffn_wq_up, L0_ffn_s_up,
           L0_ffn_wq_down, L0_ffn_s_down):
    cfg = FULL
    x = np.asarray(L0_attn_x)
    n_cores = x.shape[0]
    nc, c = build_nc(cfg)
    common = None
    in_maps = []
    for b in range(n_cores):
        m = prep_inputs(cfg, x[b],
                        np.asarray(L0_attn_rms_w), np.asarray(L0_attn_wq),
                        np.asarray(L0_attn_bq), np.asarray(L0_attn_wk),
                        np.asarray(L0_attn_bk), np.asarray(L0_attn_wv),
                        np.asarray(L0_attn_bv), np.asarray(L0_attn_wo),
                        np.asarray(L0_attn_bo), np.asarray(L0_ffn_wq_gate),
                        np.asarray(L0_ffn_s_gate), np.asarray(L0_ffn_wq_up),
                        np.asarray(L0_ffn_s_up), np.asarray(L0_ffn_wq_down),
                        np.asarray(L0_ffn_s_down)) if common is None else dict(common)
        if common is None:
            common = {k: v for k, v in m.items() if k != "xT"}
        m["xT"] = np.ascontiguousarray(x[b].T).astype(np.float32)
        in_maps.append(m)
    res = run_bass_kernel_spmd(nc, in_maps, core_ids=list(range(n_cores)))
    outs = [np.ascontiguousarray(np.asarray(r["out"]).T) for r in res.results]
    return np.stack(outs, axis=0)


# version 24
# speedup vs baseline: 1.3389x; 1.3389x over previous
import numpy as np
import ml_dtypes
from contextlib import ExitStack

import concourse.bass as bass
import concourse.mybir as mybir
from concourse.bass_utils import run_bass_kernel_spmd

F32 = mybir.dt.float32
BF16 = mybir.dt.bfloat16
I32 = mybir.dt.int32
AF = mybir.ActivationFunctionType
ALU = mybir.AluOpType
AX = mybir.AxisListType

NEG = -3.0e38
INV_SQRT_HD = 2.0 / np.sqrt(128.0)
UC = 40.0


class CFG:
    def __init__(self, D=2048, T=2048, HID=5632, TtF=1024, TtK=512, QG=256):
        self.D, self.T, self.HID = D, T, HID
        self.NDC = D // 128
        self.NH = D // 128
        self.NHC = HID // 128
        self.TtF, self.TtK, self.QG = TtF, TtK, QG
        self.theta = 10000.0
        self.eps = 1e-5


FULL = CFG(TtK=512)


def slices(width, step=512):
    out = []
    o = 0
    while o < width:
        w = min(step, width - o)
        out.append((o, w))
        o += w
    return out



class Op:
    __slots__ = ("idx", "stream", "kind", "fn", "deps", "inc_sem", "val", "waits", "tag")

    def __init__(self, idx, stream, kind, fn, tag):
        self.idx, self.stream, self.kind, self.fn, self.tag = idx, stream, kind, fn, tag
        self.deps = set()
        self.inc_sem = None
        self.val = None
        self.waits = []


class Prog:
    CSTREAMS = ("pe", "act", "dve", "pool")

    NLANES = {"qs": 8, "qg": 6, "qa": 4}

    def __init__(self):
        self.ops = []
        self.last_w = {}
        self.readers = {}
        self.last_real = {}
        self.dcnt = {"qs": 0, "qg": 0, "qa": 0}
        self.lane_prev = {}

    def add(self, stream, fn, reads=(), writes=(), kind="c", tag=""):
        idx = len(self.ops)
        op = Op(idx, stream, kind, fn, tag)
        for k in reads:
            d = self.last_w.get(k)
            if d is not None:
                op.deps.add(d)
        for k in writes:
            d = self.last_w.get(k)
            if d is not None:
                op.deps.add(d)
            for r in self.readers.get(k, ()):
                op.deps.add(r)
        op.deps = {d for d in op.deps
                   if not (stream == "pe" and self.ops[d].stream == "pe")}
        for k in writes:
            self.last_w[k] = idx
            self.readers[k] = []
        for k in reads:
            self.readers.setdefault(k, []).append(idx)
        self.ops.append(op)
        self.last_real[stream] = idx
        return idx

    def dma(self, stream, fn, reads=(), writes=(), tag=""):
        queue = {"pool": "qg", "act": "qa"}.get(stream, "qs")
        j = self.dcnt[queue]
        self.dcnt[queue] = j + 1
        nl = self.NLANES[queue]
        lane = f"{queue}{j % nl}"
        idx = self.add(stream, fn, reads, writes, kind="d", tag=tag)
        op = self.ops[idx]
        op.inc_sem = lane
        op.val = 16 * (j // nl + 1)
        prev = self.lane_prev.get(lane)
        if prev is not None:
            op.deps.add(prev)
        self.lane_prev[lane] = idx
        return idx

    def barrier(self, tag="bar"):
        dep_set = set(self.last_real.values()) | set(self.lane_prev.values())
        for s in ("pe", "act", "dve", "pool", "qs"):
            idx = len(self.ops)
            op = Op(idx, s, "w", None, tag)
            op.deps = set(dep_set)
            self.ops.append(op)

    def finalize(self, block, sems):
        need_inc = set()
        for op in self.ops:
            for d in op.deps:
                need_inc.add(d)
        cnt = {k: 0 for k in sems}
        for op in self.ops:
            if op.kind == "c" and op.idx in need_inc:
                sem = op.stream
                cnt[sem] += 1
                op.inc_sem, op.val = sem, cnt[sem]
        observed = {s: {} for s in ("pe", "act", "dve", "pool", "qs")}
        snap = {}
        for op in self.ops:
            obs = observed[op.stream]
            for d in sorted(op.deps):
                dop = self.ops[d]
                if dop.inc_sem is None:
                    continue
                if obs.get(dop.inc_sem, 0) >= dop.val:
                    continue
                op.waits.append((dop.inc_sem, dop.val))
                dsnap = snap.get(d)
                if dsnap:
                    for k, v in dsnap.items():
                        if obs.get(k, 0) < v:
                            obs[k] = v
                if obs.get(dop.inc_sem, 0) < dop.val:
                    obs[dop.inc_sem] = dop.val
            if op.inc_sem is not None:
                snap[op.idx] = dict(obs)
        per = {s: [] for s in ("pe", "act", "dve", "pool", "qs")}
        for op in self.ops:
            per[op.stream].append(op)
        stats = {s: (len(per[s]), sum(len(o.waits) for o in per[s])) for s in per}

        def replay(stream):
            def go(eng):
                for op in per[stream]:
                    for sem, val in op.waits:
                        eng.wait_ge(sems[sem], val)
                    if op.fn is not None:
                        inst = op.fn(eng)
                        if op.inc_sem is not None:
                            inst.then_inc(sems[op.inc_sem], 16 if op.kind == "d" else 1)
            return go

        block.tensor(replay("pe"))
        block.scalar(replay("act"))
        block.vector(replay("dve"))
        block.gpsimd(replay("pool"))
        block.sync(replay("qs"))
        return stats



def rkeys(name, dc, t0, t1):
    return [f"{name}:{dc}:{b}" for b in range(t0 // 256, (t1 + 255) // 256)]


def pbanks(name, off, width):
    return [f"{name}:b{b}" for b in range(off // 512, (off + width + 511) // 512)]


class Ctx:
    pass


def build_nc(cfg, phases=("ffn1", "attn", "ffn2"), ffn1_convert=True):
    nc = bass.Bass("TRN2", target_bir_lowering=False)
    c = Ctx()
    c.cfg = cfg
    c.nc = nc
    P = Prog()
    c.P = P
    D, T, HID = cfg.D, cfg.T, cfg.HID
    NDC, NH, NHC = cfg.NDC, cfg.NH, cfg.NHC

    dt = lambda n, s, ty, k="ExternalInput": nc.dram_tensor(n, s, ty, kind=k).ap()
    c.xT = dt("xT", [D, T], F32)
    c.wq_t = dt("wq_t", [NH * 128, NDC * 128], F32)
    c.wk_t = dt("wk_t", [NH * 128, NDC * 128], F32)
    c.wv_t = dt("wv_t", [NH * 128, NDC * 128], F32)
    c.wo_t = dt("wo_t", [NDC * 128, NH * 128], F32)
    c.bq_col = dt("bq_col", [128, NH], F32)
    c.bk_col = dt("bk_col", [128, NH], F32)
    c.bv_col = dt("bv_col", [128, NH], F32)
    c.bo_col = dt("bo_col", [128, NDC], F32)
    c.gam_col = dt("gam_col", [128, NDC], F32)
    c.wg_t = dt("wg_t", [NHC * 128, NDC * 128], I32)
    c.wu_t = dt("wu_t", [NHC * 128, NDC * 128], I32)
    c.wd_t = dt("wd_t", [NDC * 128, NHC * 128], I32)
    c.sg_col = dt("sg_col", [128, NHC], F32)
    c.su_col = dt("su_col", [128, NHC], F32)
    c.sd_col = dt("sd_col", [128, NDC], F32)
    c.ce_t = dt("ce_t", [128, T], BF16)
    c.se_t = dt("se_t", [128, T], BF16)
    c.maskw_s = dt("maskw_s", [128, 640], BF16)
    c.mask_c1 = dt("mask_c1", [128, cfg.QG], BF16)
    c.mask_c2 = dt("mask_c2", [128, cfg.QG], BF16)
    c.ident_f = dt("ident_f", [128, 128], F32)
    c.ident_bf = dt("ident_bf", [128, 128], BF16)
    c.R1 = dt("R1", [D, T], F32, "Internal")
    c.R2 = dt("R2", [D, T], F32, "Internal")
    c.wg_sc = dt("wg_sc", [NHC * 128, NDC * 128], BF16, "Internal")
    c.wu_sc = dt("wu_sc", [NHC * 128, NDC * 128], BF16, "Internal")
    c.wd_sc = dt("wd_sc", [NDC * 128, NHC * 128], BF16, "Internal")
    c.wq_sc = dt("wq_sc", [NH * 128, NDC * 128], BF16, "Internal")
    c.wk_sc = dt("wk_sc", [NH * 128, NDC * 128], BF16, "Internal")
    c.wv_sc = dt("wv_sc", [NH * 128, NDC * 128], BF16, "Internal")
    c.wo_sc = dt("wo_sc", [NDC * 128, NH * 128], BF16, "Internal")
    c.out = dt("out", [D, T], F32, "ExternalOutput")

    es = ExitStack()
    with es:
        sb = lambda n, s, ty: es.enter_context(nc.sbuf_tensor(n, s, ty))
        c.ones_col_bf = sb("ones_col_bf", [128, 1], BF16)
        c.ones_row_f = sb("ones_row_f", [33, 128], F32)
        c.ones_row_bf = sb("ones_row_bf", [1, 128], BF16)
        c.s_ident_f = sb("s_ident_f", [128, 128], F32)
        c.s_ident_bf = sb("s_ident_bf", [128, 128], BF16)
        c.s_maskw = sb("s_maskw", [128, 640], BF16)
        c.s_mc1 = sb("s_mc1", [128, cfg.QG], BF16)
        c.s_mc2 = sb("s_mc2", [128, cfg.QG], BF16)
        c.s_bq = sb("s_bq", [128, NH], F32)
        c.s_bk = sb("s_bk", [128, NH], F32)
        c.s_bv = sb("s_bv", [128, NH], F32)
        c.s_bo = sb("s_bo", [128, NDC], F32)
        c.s_gam = sb("s_gam", [128, NDC], F32)
        c.s_sg = sb("s_sg", [128, NHC], F32)
        c.s_su = sb("s_su", [128, NHC], F32)
        c.s_sde = sb("s_sde", [128, NDC], F32)

        sems = {}
        sem_names = ["pe", "act", "dve", "pool"]
        sem_names += [f"qs{i}" for i in range(Prog.NLANES["qs"])]
        sem_names += [f"qg{i}" for i in range(Prog.NLANES["qg"])]
        sem_names += [f"qa{i}" for i in range(Prog.NLANES["qa"])]
        for name in sem_names:
            sems[name] = es.enter_context(nc.semaphore("sem_" + name))

        emit_consts(c)
        if "ffn1" in phases:
            with ExitStack() as ph:
                F = alloc_ffn(c, ph, es)
                emit_ffn(c, F, src=c.xT, srcn="xT", dst=c.R1, dstn="R1", convert=ffn1_convert)
            P.barrier("p1end")
        if "attn" in phases:
            with ExitStack() as ph:
                alloc_attn(c, ph)
                emit_group, finish_groups = make_attn_groups(c)
                gpt = max(cfg.TtK // cfg.QG, 1)

                def after_tile(kt):
                    for g in range(kt * gpt, (kt + 1) * gpt):
                        if g < cfg.T // cfg.QG:
                            emit_group(g)

                emit_kv_prep(c, after_tile=after_tile)
                finish_groups()
            P.barrier("p2end")
        if "ffn2" in phases:
            with ExitStack() as ph:
                F = alloc_ffn(c, ph, es)
                emit_ffn(c, F, src=c.R2, srcn="R2", dst=c.out, dstn="out", convert=False)
        P.barrier("final")

        block = es.enter_context(nc.Block())
        c.stats = P.finalize(block, sems)
    return nc, c


def emit_consts(c):
    P, nc, cfg = c.P, c.nc, c.cfg
    loads = [
        (c.s_bq, c.bq_col, "s_bq"), (c.s_bk, c.bk_col, "s_bk"),
        (c.s_bv, c.bv_col, "s_bv"), (c.s_bo, c.bo_col, "s_bo"),
        (c.s_gam, c.gam_col, "s_gam"), (c.s_sg, c.sg_col, "s_sg"),
        (c.s_su, c.su_col, "s_su"),
        (c.s_ident_f, c.ident_f, "s_ident_f"), (c.s_ident_bf, c.ident_bf, "s_ident_bf"),
        (c.s_maskw, c.maskw_s, "s_maskw"), (c.s_mc1, c.mask_c1, "s_mc1"),
        (c.s_mc2, c.mask_c2, "s_mc2"),
    ]
    for dstt, src, key in loads:
        P.dma("qs", lambda e, d=dstt, s=src: e.dma_start(d[:], s), writes=[key])
    P.dma("qs", lambda e: e.dma_start(c.s_sde[:], c.sd_col), writes=["s_sde"])
    P.add("dve", lambda e: e.tensor_scalar(c.s_sde[:], c.s_sde[:], 0.5, None, ALU.mult),
          reads=["s_sde"], writes=["s_sde"])
    P.add("pool", lambda e: e.memset(c.ones_col_bf[:], 1.0), writes=["ones_col_bf"])
    P.add("pool", lambda e: e.memset(c.ones_row_f[:], 1.0), writes=["ones_row_f"])
    P.add("pool", lambda e: e.memset(c.ones_row_bf[:], 1.0), writes=["ones_row_bf"])


def alloc_ffn(c, ph, es):
    from types import SimpleNamespace
    nc, cfg = c.nc, c.cfg
    sb = lambda n, s, ty: ph.enter_context(nc.sbuf_tensor(n, s, ty))
    pt = lambda n, s, ty: ph.enter_context(nc.psum_tensor(n, s, ty))
    NDC, NHC, TtF = cfg.NDC, cfg.NHC, cfg.TtF
    F = SimpleNamespace()
    u = alloc_ffn.count = getattr(alloc_ffn, "count", 0) + 1
    F.rbf = [sb(f"f{u}_rbf{i}", [128, TtF], BF16) for i in range(NDC)]
    F.g = [sb(f"f{u}_g{i}", [128, TtF], BF16) for i in range(NHC)]
    F.rsq = [sb(f"f{u}_rsq{i}", [128, TtF], BF16) for i in range(2)]
    F.gt = [sb(f"f{u}_gt{i}", [128, TtF], F32) for i in range(2)]
    F.rwb = [sb(f"f{u}_rwb{i}", [1, TtF], BF16) for i in range(2)]
    F.wg = [sb(f"f{u}_wg{i}", [128, NDC * 128], BF16) for i in range(2)]
    F.wu = [sb(f"f{u}_wu{i}", [128, NDC * 128], BF16) for i in range(2)]
    F.wd = [sb(f"f{u}_wd{i}", [128, NHC * 128], BF16) for i in range(2)]
    F.rc2 = [sb(f"f{u}_rc2{i}", [128, TtF], F32) for i in range(2)]
    F.Br = sb(f"f{u}_Br", [128, TtF], BF16)
    F.Bm = sb(f"f{u}_Bm", [128, TtF], BF16)
    F.rows = [sb(f"f{u}_row{i}", [1, TtF], F32) for i in range(3)]
    F.pp = [pt(f"f{u}_pp{i}", [128, TtF], F32) for i in range(4)]
    F.cvt = [sb(f"f{u}_cvt{i}", [128, (NDC * 128) // 2], BF16) for i in range(1)]
    return F


def conv_tasks_ffn(c):
    return None


def emit_conv_attn(c, F, tasks, n):
    P, cfg = c.P, c.cfg
    for _ in range(n):
        if not tasks:
            return
        src, dst, row, key = tasks.pop(0)
        buf = F.cvt[0]
        bk = "cvtbuf0"
        W = src.tensor.shape[1]
        hw = W // 2
        for pc in range(2):
            P.dma("pool", lambda e, b=buf, s=src, r=row, o=pc * hw, hw=hw: e.dma_start(
                b[:, 0:hw], s[r:r + 128, o:o + hw]),
                reads=[], writes=[bk], tag="cvA")
            P.dma("act", lambda e, b=buf, d=dst, r=row, o=pc * hw, hw=hw: e.dma_start(
                d[r:r + 128, o:o + hw], b[:, 0:hw]),
                reads=[bk], writes=[key], tag="cvAw")


def emit_ffn(c, F, src, srcn, dst, dstn, convert):
    P, nc, cfg = c.P, c.nc, c.cfg
    D, T, HID, TtF = cfg.D, cfg.T, cfg.HID, cfg.TtF
    NDC, NHC = cfg.NDC, cfg.NHC
    nsl = slices(TtF)

    atasks = []
    if convert:
        for oc in range(cfg.NH):
            atasks.append((c.wk_t, c.wk_sc, oc * 128, f"wk:{oc}"))
            atasks.append((c.wv_t, c.wv_sc, oc * 128, f"wv:{oc}"))
        for oc in range(cfg.NH):
            atasks.append((c.wq_t, c.wq_sc, oc * 128, f"wq:{oc}"))
        for doc in range(NDC):
            atasks.append((c.wo_t, c.wo_sc, doc * 128, f"wo:{doc}"))

    for ti in range(T // TtF):
        t0 = ti * TtF
        pp = F.pp
        for dc in range(NDC):
            r = dc % 2
            P.dma("pool", lambda e, dc=dc, t0=t0: e.dma_start(
                F.rbf[dc][:], src[dc * 128:(dc + 1) * 128, t0:t0 + TtF]),
                reads=rkeys(srcn, dc, t0, t0 + TtF), writes=[f"frbf{dc}"], tag="lnR")
            P.add("dve", lambda e, dc=dc, r=r: e.tensor_mul(
                F.rsq[r][:], F.rbf[dc][:], F.rbf[dc][:]),
                reads=[f"frbf{dc}"], writes=[f"frsq{r}"])
            for (o, w) in nsl:
                P.add("pe", lambda e, dc=dc, o=o, w=w: e.matmul(
                    pp[0][0:1, o:o + w], c.ones_col_bf[:], F.rbf[dc][:, o:o + w],
                    start=(dc == 0), stop=(dc == NDC - 1)),
                    reads=["ones_col_bf", f"frbf{dc}"], writes=pbanks("fpp0", o, w))
                P.add("pe", lambda e, dc=dc, r=r, o=o, w=w: e.matmul(
                    pp[1][0:1, o:o + w], c.ones_col_bf[:], F.rsq[r][:, o:o + w],
                    start=(dc == 0), stop=(dc == NDC - 1)),
                    reads=["ones_col_bf", f"frsq{r}"], writes=pbanks("fpp1", o, w))
        rw = F.rows
        allp0 = pbanks("fpp0", 0, TtF)
        allp1 = pbanks("fpp1", 0, TtF)
        P.add("dve", lambda e: e.tensor_scalar(rw[0][:], pp[0][0:1, :], 1.0 / D, None, ALU.mult),
              reads=allp0, writes=["frow0"])
        P.add("dve", lambda e: e.tensor_scalar(rw[1][:], pp[1][0:1, :], 1.0 / D, cfg.eps,
                                               ALU.mult, ALU.add),
              reads=allp1, writes=["frow1"])
        P.add("dve", lambda e: e.tensor_mul(rw[2][:], rw[0][:], rw[0][:]),
              reads=["frow0"], writes=["frow2"])
        P.add("dve", lambda e: e.tensor_sub(rw[1][:], rw[1][:], rw[2][:]),
              reads=["frow1", "frow2"], writes=["frow1"])
        P.add("act", lambda e: e.activation(rw[2][:], rw[1][:], AF.Ln),
              reads=["frow1"], writes=["frow2"])
        P.add("act", lambda e: e.activation(rw[1][:], rw[2][:], AF.Exp, scale=-0.5),
              reads=["frow2"], writes=["frow1"])
        P.add("dve", lambda e: e.tensor_copy(F.rwb[0][:], rw[1][:]),
              reads=["frow1"], writes=["frwb0"])
        P.add("dve", lambda e: e.tensor_mul(rw[0][:], rw[0][:], rw[1][:]),
              reads=["frow0", "frow1"], writes=["frow0"])
        P.add("dve", lambda e: e.tensor_copy(F.rwb[1][:], rw[0][:]),
              reads=["frow0"], writes=["frwb1"])
        for (o, w) in nsl:
            P.add("pe", lambda e, o=o, w=w: e.matmul(
                pp[2][:, o:o + w], c.ones_row_bf[:], F.rwb[0][0:1, o:o + w],
                start=True, stop=True),
                reads=["ones_row_bf", "frwb0"], writes=pbanks("fpp2", o, w))
            P.add("pe", lambda e, o=o, w=w: e.matmul(
                pp[3][:, o:o + w], c.ones_row_bf[:], F.rwb[1][0:1, o:o + w],
                start=True, stop=True),
                reads=["ones_row_bf", "frwb1"], writes=pbanks("fpp3", o, w))
        P.add("act", lambda e: e.activation(F.Br[:], pp[2][:], AF.Copy),
              reads=pbanks("fpp2", 0, TtF), writes=["fBr"])
        P.add("act", lambda e: e.activation(F.Bm[:], pp[3][:], AF.Copy),
              reads=pbanks("fpp3", 0, TtF), writes=["fBm"])
        for dc in range(NDC):
            P.add("dve", lambda e, dc=dc: e.tensor_mul(
                F.rbf[dc][:], F.rbf[dc][:], F.Br[:]),
                reads=[f"frbf{dc}", "fBr"], writes=[f"frbf{dc}"])
            P.add("dve", lambda e, dc=dc: e.tensor_sub(
                F.rbf[dc][:], F.rbf[dc][:], F.Bm[:]),
                reads=[f"frbf{dc}", "fBm"], writes=[f"frbf{dc}"])

        hcols = NDC * 64
        for hc in range(NHC):
            par = hc % 2
            gw, uw = F.wg[par], F.wu[par]
            if convert and ti == 0:
                for hx in range(2):
                    P.dma("pool", lambda e, hc=hc, b=gw, o=hx * hcols: e.dma_start(
                        b[:, o:o + hcols],
                        c.wg_t[hc * 128:(hc + 1) * 128, o:o + hcols]),
                        writes=[f"fwg{par}h{hx}"], tag="cvG")
                    P.add("dve", lambda e, b=gw, o=hx * hcols: e.tensor_scalar(
                        b[:, o:o + hcols], b[:, o:o + hcols], -128.0, None, ALU.add),
                        reads=[f"fwg{par}h{hx}"], writes=[f"fwg{par}h{hx}"])
                    P.dma("act", lambda e, hc=hc, b=gw, o=hx * hcols: e.dma_start(
                        c.wg_sc[hc * 128:(hc + 1) * 128, o:o + hcols],
                        b[:, o:o + hcols]),
                        reads=[f"fwg{par}h{hx}"], writes=[f"wg:{hc}"])
                for hx in range(2):
                    P.dma("pool", lambda e, hc=hc, b=uw, o=hx * hcols: e.dma_start(
                        b[:, o:o + hcols],
                        c.wu_t[hc * 128:(hc + 1) * 128, o:o + hcols]),
                        writes=[f"fwu{par}h{hx}"], tag="cvU")
                    P.add("dve", lambda e, b=uw, o=hx * hcols: e.tensor_scalar(
                        b[:, o:o + hcols], b[:, o:o + hcols], -128.0, None, ALU.add),
                        reads=[f"fwu{par}h{hx}"], writes=[f"fwu{par}h{hx}"])
                    P.dma("act", lambda e, hc=hc, b=uw, o=hx * hcols: e.dma_start(
                        c.wu_sc[hc * 128:(hc + 1) * 128, o:o + hcols],
                        b[:, o:o + hcols]),
                        reads=[f"fwu{par}h{hx}"], writes=[f"wu:{hc}"])
            else:
                P.dma("qs", lambda e, hc=hc, b=gw: e.dma_start(
                    b[:], c.wg_sc[hc * 128:(hc + 1) * 128, :]),
                    reads=[f"wg:{hc}"], writes=[f"fwg{par}h0", f"fwg{par}h1"])
                P.dma("qs", lambda e, hc=hc, b=uw: e.dma_start(
                    b[:], c.wu_sc[hc * 128:(hc + 1) * 128, :]),
                    reads=[f"wu:{hc}"], writes=[f"fwu{par}h0", f"fwu{par}h1"])
                if convert:
                    emit_conv_attn(c, F, atasks, 1)
            ppg, ppu = pp[par], pp[2 + par]
            gk, uk = f"fpp{par}", f"fpp{2+par}"
            for dc in range(NDC):
                hx = dc // (NDC // 2)
                for (o, w) in nsl:
                    P.add("pe", lambda e, dc=dc, o=o, w=w, ppg=ppg, gw=gw, hx=hx:
                          e.matmul(
                        ppg[:, o:o + w], gw[:, dc * 128:(dc + 1) * 128],
                        F.rbf[dc][:, o:o + w],
                        start=(dc == 0), stop=(dc == NDC - 1)),
                        reads=[f"fwg{par}h{hx}", f"frbf{dc}"], writes=pbanks(gk, o, w))
            for dc in range(NDC):
                hx = dc // (NDC // 2)
                for (o, w) in nsl:
                    P.add("pe", lambda e, dc=dc, o=o, w=w, ppu=ppu, uw=uw, hx=hx:
                          e.matmul(
                        ppu[:, o:o + w], uw[:, dc * 128:(dc + 1) * 128],
                        F.rbf[dc][:, o:o + w],
                        start=(dc == 0), stop=(dc == NDC - 1)),
                        reads=[f"fwu{par}h{hx}", f"frbf{dc}"], writes=pbanks(uk, o, w))
            gt = F.gt[par]
            P.add("act", lambda e, hc=hc, ppg=ppg, gt=gt: e.activation(
                gt[:], ppg[:], AF.Silu, scale=c.s_sg[:, hc:hc + 1]),
                reads=pbanks(gk, 0, TtF) + ["s_sg"], writes=[f"fgt{par}"])
            P.add("dve", lambda e, hc=hc, ppu=ppu, gt=gt: e.scalar_tensor_tensor(
                F.g[hc][:], ppu[:], c.s_su[:, hc:hc + 1], gt[:], ALU.mult, ALU.mult),
                reads=pbanks(uk, 0, TtF) + ["s_su", f"fgt{par}"], writes=[f"fg{hc}"])

        for doc in range(NDC):
            par = doc % 2
            dw = F.wd[par]
            qcols = (NHC * 128) // 4
            if convert and ti == 0:
                for qx in range(4):
                    P.dma("pool", lambda e, doc=doc, b=dw, o=qx * qcols: e.dma_start(
                        b[:, o:o + qcols],
                        c.wd_t[doc * 128:(doc + 1) * 128, o:o + qcols]),
                        writes=[f"fwd{par}q{qx}"], tag="cvD")
                    P.add("dve", lambda e, b=dw, o=qx * qcols: e.tensor_scalar(
                        b[:, o:o + qcols], b[:, o:o + qcols], -128.0, None, ALU.add),
                        reads=[f"fwd{par}q{qx}"], writes=[f"fwd{par}q{qx}"])
                    P.dma("act", lambda e, doc=doc, b=dw, o=qx * qcols: e.dma_start(
                        c.wd_sc[doc * 128:(doc + 1) * 128, o:o + qcols],
                        b[:, o:o + qcols]),
                        reads=[f"fwd{par}q{qx}"], writes=[f"wd:{doc}"])
            else:
                P.dma("qs", lambda e, doc=doc, b=dw: e.dma_start(
                    b[:], c.wd_sc[doc * 128:(doc + 1) * 128, :]),
                    reads=[f"wd:{doc}"],
                    writes=[f"fwd{par}q{qx}" for qx in range(4)])
                if convert:
                    emit_conv_attn(c, F, atasks, 2)
            P.dma("qs", lambda e, doc=doc, par=par, t0=t0: e.dma_start(
                F.rc2[par][:], src[doc * 128:(doc + 1) * 128, t0:t0 + TtF]),
                reads=rkeys(srcn, doc, t0, t0 + TtF), writes=[f"frc2{par}"])
            ppd = pp[par]
            dk = f"fpp{par}"
            for hc in range(NHC):
                qx = hc // (NHC // 4)
                for (o, w) in nsl:
                    P.add("pe", lambda e, hc=hc, o=o, w=w, ppd=ppd, dw=dw, qx=qx:
                          e.matmul(
                        ppd[:, o:o + w], dw[:, hc * 128:(hc + 1) * 128],
                        F.g[hc][:, o:o + w],
                        start=(hc == 0), stop=(hc == NHC - 1)),
                        reads=[f"fwd{par}q{qx}", f"fg{hc}"], writes=pbanks(dk, o, w))
            rc2 = F.rc2[par]
            P.add("dve", lambda e, doc=doc, ppd=ppd, rc2=rc2, par=par: e.scalar_tensor_tensor(
                rc2[:], ppd[:], c.s_sde[:, doc:doc + 1], rc2[:], ALU.mult, ALU.add),
                reads=pbanks(dk, 0, TtF) + ["s_sde", f"frc2{par}"], writes=[f"frc2{par}"])
            P.dma("act", lambda e, doc=doc, rc2=rc2, t0=t0: e.dma_start(
                dst[doc * 128:(doc + 1) * 128, t0:t0 + TtF], rc2[:]),
                reads=[f"frc2{par}"], writes=rkeys(dstn, doc, t0, t0 + TtF))
    if convert:
        emit_conv_attn(c, F, atasks, len(atasks))


def alloc_attn(c, ph):
    nc, cfg = c.nc, c.cfg
    sb = lambda n, s, ty: ph.enter_context(nc.sbuf_tensor(n, s, ty))
    pt = lambda n, s, ty: ph.enter_context(nc.psum_tensor(n, s, ty))
    D, T, NDC, NH = cfg.D, cfg.T, cfg.NDC, cfg.NH
    TtK, QG = cfg.TtK, cfg.QG
    TN = min(256, TtK)
    c.a_ce = [sb(f"a_ce{i}", [128, TtK], BF16) for i in range(2)]
    c.a_se = [sb(f"a_se{i}", [128, TtK], BF16) for i in range(2)]
    c.a_K = [sb(f"a_K{i}", [128, T], BF16) for i in range((NH + 1) // 2)]
    c.a_V = sb("a_V", [128, (T // 128) * D], BF16)
    c.a_xk = [sb(f"a_xk{i}", [128, TtK], BF16) for i in range(NDC)]
    c.a_rst = [sb(f"a_rst{i}", [128, TN], BF16) for i in range(4)]
    c.a_rsq = [sb(f"a_rsq{i}", [128, TN], BF16) for i in range(2)]
    c.a_Bs = [sb(f"a_Bs{i}", [128, TN], BF16) for i in range(2)]
    c.a_Bm = [sb(f"a_Bm{i}", [128, TN], BF16) for i in range(2)]
    c.a_rowFt = [sb(f"a_rowFt{k}", [64, 2 * TN], F32) for k in range(2)]
    c.a_rowF = [[c.a_rowFt[k][(s % 2) * 32:(s % 2) * 32 + 1,
                              (s // 2) * TN:(s // 2) * TN + TN]
                 for s in range(4)] for k in range(2)]
    c.a_BmF = [sb(f"a_BmF{i}", [128, QG], F32) for i in range(2)]
    c.a_Br1F = [sb(f"a_Br1F{i}", [128, QG], F32) for i in range(2)]
    c.a_rows = [sb(f"a_row{i}", [1, TN], F32) for i in range(6)]
    c.a_rwb = [sb(f"a_rwb{i}", [1, TN], BF16) for i in range(2)]
    c.a_z = [sb(f"a_z{i}", [128, QG], F32) for i in range(2)]
    c.a_kf = [sb(f"a_kf{i}", [128, TtK], BF16) for i in range(4)]
    c.a_rt = [sb(f"a_rt{i}", [128, TtK], BF16) for i in range(2)]
    c.a_wA = [sb(f"a_wA{i}", [128, NDC * 128], BF16) for i in range(2)]
    c.a_wB = [sb(f"a_wB{i}", [128, max(NDC, NH) * 128], BF16) for i in range(2)]
    c.a_vt = [sb(f"a_vt{i}", [128, TtK], BF16) for i in range(2)]
    c.a_rq = [sb(f"a_rq{i}", [128, QG], F32) for i in range(2)]
    c.a_q = [sb(f"a_q{i}", [128, QG], BF16) for i in range((NH + 1) // 2)]
    c.a_qf = [sb(f"a_qf{i}", [128, QG], BF16) for i in range(4)]
    c.a_qrt = [sb(f"a_qrt{i}", [128, QG], BF16) for i in range(2)]
    c.a_ksq = [sb("a_ksq0", [128, TtK], BF16)]
    c.a_mk2 = sb("a_mk2", [1, max(NH, 4)], F32)
    c.a_qm2 = sb("a_qm2", [1, max(NH, 4)], F32)
    c.a_ubr = [sb(f"a_ubr{i}", [1, max(NH, 4)], F32) for i in range(2)]
    c.a_ubA = sb("a_ubA", [128, max(NH, 4)], F32)
    c.a_us = sb("a_us", [1, 4], F32)
    c.a_ceq = sb("a_ceq", [128, QG], BF16)
    c.a_seq = sb("a_seq", [128, QG], BF16)
    c.a_PT = [sb(f"a_PT{i}", [128, (T // 128) * QG], BF16) for i in range(2)]
    c.a_linv = sb("a_linv", [1, QG], F32)
    c.a_linv_bf = [sb(f"a_linv_bf{i}", [1, QG], BF16) for i in range(2)]
    c.a_Blinv = sb("a_Blinv", [128, QG], BF16)
    c.a_ao = sb("a_ao", [128, NH * QG], BF16)
    c.a_oc = [sb(f"a_oc{i}", [128, QG], F32) for i in range(2)]
    c.a_pA = pt("a_pA", [128, 1024], F32)
    c.a_pB = pt("a_pB", [128, 1024], F32)
    c.a_pPV = pt("a_pPV", [128, 256], F32)
    c.a_pL = pt("a_pL", [1, 512], F32)
    c.a_pC = pt("a_pC", [128, 512], F32)
    c.a_pT = pt("a_pT", [128, 512], BF16)


def emit_rope(c, srcA, srcAk, srcB, srcBk, dst_tensor, dstks, t0, W, qmode,
              ceb, cek, seb, sek):
    P = c.P
    rt = c.a_qrt if qmode else c.a_rt
    rk = "aqrt" if qmode else "art"
    P.add("dve", lambda e: e.tensor_mul(rt[0][:, 0:W], ceb[:, 0:W], srcA[:, 0:W]),
          reads=[cek, srcAk], writes=[rk + "0"])
    P.add("dve", lambda e: e.tensor_mul(rt[1][:, 0:W], seb[:, 0:W], srcB[:, 0:W]),
          reads=[sek, srcBk], writes=[rk + "1"])
    P.add("dve", lambda e: e.tensor_add(dst_tensor[:, :] if qmode else
                                        dst_tensor[:, t0:t0 + W],
                                        rt[0][:, 0:W], rt[1][:, 0:W]),
          reads=[rk + "0", rk + "1"], writes=dstks)


def emit_norm_rows(c, N):
    P, cfg = c.P, c.cfg
    D, eps = cfg.D, cfg.eps
    rw = c.a_rows
    A = lambda i: rw[i][0:1, 0:N]
    P.add("dve", lambda e: e.tensor_scalar(A(0), c.a_pL[0:1, 0:N], 1.0 / D, None, ALU.mult),
          reads=["apL"], writes=["arow0"])
    P.add("dve", lambda e: e.tensor_scalar(A(1), c.a_pC[0:1, 0:N], 1.0 / D, eps,
                                           ALU.mult, ALU.add),
          reads=["apC"], writes=["arow1"])
    P.add("dve", lambda e: e.tensor_mul(A(2), A(0), A(0)), reads=["arow0"], writes=["arow2"])
    P.add("dve", lambda e: e.tensor_sub(A(1), A(1), A(2)),
          reads=["arow1", "arow2"], writes=["arow1"])
    P.add("act", lambda e: e.activation(A(2), A(1), AF.Ln), reads=["arow1"], writes=["arow2"])
    P.add("act", lambda e: e.activation(A(3), A(2), AF.Exp, scale=-0.5),
          reads=["arow2"], writes=["arow3"])
    P.add("dve", lambda e: e.tensor_scalar(A(4), A(1), -eps, None, ALU.add),
          reads=["arow1"], writes=["arow4"])
    P.add("dve", lambda e: e.reciprocal(A(5), A(1)), reads=["arow1"], writes=["arow5"])
    P.add("dve", lambda e: e.tensor_mul(A(4), A(4), A(5)),
          reads=["arow4", "arow5"], writes=["arow4"])
    P.add("dve", lambda e: e.tensor_scalar(A(4), A(4), cfg.eps, None, ALU.add),
          reads=["arow4"], writes=["arow4"])
    P.add("act", lambda e: e.activation(A(5), A(4), AF.Ln), reads=["arow4"], writes=["arow5"])
    P.add("act", lambda e: e.activation(A(5), A(5), AF.Exp, scale=-0.5),
          reads=["arow5"], writes=["arow5"])
    P.add("dve", lambda e: e.tensor_mul(A(4), A(3), A(5)),
          reads=["arow3", "arow5"], writes=["arow4"])
    P.add("dve", lambda e: e.tensor_copy(c.a_rwb[0][0:1, 0:N], A(0)),
          reads=["arow0"], writes=["arwb0"])
    P.add("dve", lambda e: e.tensor_copy(c.a_rwb[1][0:1, 0:N], A(4)),
          reads=["arow4"], writes=["arwb1"])


def emit_norm_stats(c, srcname, bufs, bufkeyf, t0, N):
    P, cfg = c.P, c.cfg
    NDC = cfg.NDC
    nsl = slices(N)
    for dc in range(NDC):
        r = dc % 2
        b = bufs[dc % len(bufs)]
        bk = bufkeyf(dc)
        P.dma("pool", lambda e, dc=dc, b=b, N=N, t0=t0: e.dma_start(
            b[:, 0:N], c.R1[dc * 128:(dc + 1) * 128, t0:t0 + N]),
            reads=rkeys("R1", dc, t0, t0 + N), writes=[bk])
        P.add("dve", lambda e, b=b, r=r, N=N: e.tensor_mul(
            c.a_rsq[r][:, 0:N], b[:, 0:N], b[:, 0:N]),
            reads=[bk], writes=[f"arsq{r}"])
        for (o, w) in nsl:
            P.add("pe", lambda e, b=b, o=o, w=w, dc=dc: e.matmul(
                c.a_pL[0:1, o:o + w], c.ones_col_bf[:], b[:, o:o + w],
                start=(dc == 0), stop=(dc == NDC - 1)),
                reads=["ones_col_bf", bk], writes=["apL"])
            P.add("pe", lambda e, r=r, o=o, w=w, dc=dc: e.matmul(
                c.a_pC[0:1, o:o + w], c.ones_col_bf[:], c.a_rsq[r][:, o:o + w],
                start=(dc == 0), stop=(dc == NDC - 1)),
                reads=["ones_col_bf", f"arsq{r}"], writes=["apC"])


def emit_bcast(c, rowap, rowkey, dstap, dstkey, N, f32=False, o0=0, pb=0):
    P = c.P
    ones = c.ones_row_f[pb:pb + 1, :] if f32 else c.ones_row_bf[:]
    okey = "ones_row_f" if f32 else "ones_row_bf"
    for (o, w) in slices(N):
        P.add("pe", lambda e, o=o, w=w: e.matmul(
            c.a_pC[:, o:o + w], ones, rowap[0:1, o0 + o:o0 + o + w],
            start=True, stop=True),
            reads=[okey, rowkey], writes=["apC"])
    P.add("act", lambda e: e.activation(dstap[:, 0:N], c.a_pC[:, 0:N], AF.Copy),
          reads=["apC"], writes=[dstkey])


def emit_kv_prep(c, after_tile=None):
    P, cfg = c.P, c.cfg
    D, T, NDC, NH, TtK = c.cfg.D, c.cfg.T, c.cfg.NDC, c.cfg.NH, c.cfg.TtK
    nkt = T // TtK
    TN = min(256, TtK)
    nsub = TtK // TN
    nhalf = nkt * nsub

    def stats(j):
        emit_norm_stats(c, "R1", c.a_rst, lambda dc: f"arst{dc%4}", j * TN, TN)

    def rowsbc(j):
        jp = j % 2
        js = j % 4
        emit_norm_rows(c, TN)
        emit_bcast(c, c.a_rwb[0], "arwb0", c.a_Bm[jp], f"aBm{jp}", TN)
        emit_bcast(c, c.a_rwb[1], "arwb1", c.a_Bs[jp], f"aBs{jp}", TN)
        P.add("dve", lambda e, js=js: e.tensor_copy(
            c.a_rowF[0][js][0:1, 0:TN], c.a_rows[0][0:1, 0:TN]),
            reads=["arow0"], writes=[f"arF0_{js}"])
        P.add("dve", lambda e, js=js: e.tensor_copy(
            c.a_rowF[1][js][0:1, 0:TN], c.a_rows[3][0:1, 0:TN]),
            reads=["arow3"], writes=[f"arF1_{js}"])

    def xn(j):
        t0 = j * TN
        so = (j % nsub) * TN
        jp = j % 2
        Bm, Bs = c.a_Bm[jp], c.a_Bs[jp]
        for dc in range(NDC):
            r = dc % 2
            P.dma("pool", lambda e, dc=dc, t0=t0, so=so: e.dma_start(
                c.a_xk[dc][:, so:so + TN], c.R1[dc * 128:(dc + 1) * 128, t0:t0 + TN]),
                reads=rkeys("R1", dc, t0, t0 + TN), writes=[f"axk{dc}"])
            P.add("dve", lambda e, dc=dc, r=r, so=so, Bm=Bm: e.tensor_sub(
                c.a_rsq[r][:, 0:TN], c.a_xk[dc][:, so:so + TN], Bm[:, 0:TN]),
                reads=[f"axk{dc}", f"aBm{jp}"], writes=[f"arsq{r}"])
            P.add("dve", lambda e, dc=dc, r=r, so=so, Bs=Bs: e.scalar_tensor_tensor(
                c.a_xk[dc][:, so:so + TN], c.a_rsq[r][:, 0:TN], c.s_gam[:, dc:dc + 1],
                Bs[:, 0:TN], ALU.mult, ALU.mult),
                reads=[f"arsq{r}", "s_gam", f"aBs{jp}"], writes=[f"axk{dc}"])

    def proj(kt):
        t0 = kt * TtK
        tpar = kt % 2
        ceb, seb = c.a_ce[tpar], c.a_se[tpar]
        cek, sek = f"a_ce{tpar}", f"a_se{tpar}"
        P.dma("qs", lambda e, t0=t0, ceb=ceb: e.dma_start(
            ceb[:, 0:TtK], c.ce_t[:, t0:t0 + TtK]), writes=[cek])
        P.dma("qs", lambda e, t0=t0, seb=seb: e.dma_start(
            seb[:, 0:TtK], c.se_t[:, t0:t0 + TtK]), writes=[sek])
        for i in range(NH // 2):
            kfs = []
            for AB in range(2):
                blk = AB * (NH // 2) + i
                par = (2 * i + AB) % 2
                wb = c.a_wA[par]
                for hf in range(2):
                    P.dma("qs", lambda e, blk=blk, wb=wb, hf=hf: e.dma_start(
                        wb[:, hf * NDC * 64:(hf + 1) * NDC * 64],
                        c.wk_sc[blk * 128:(blk + 1) * 128,
                                hf * NDC * 64:(hf + 1) * NDC * 64]),
                        reads=[f"wk:{blk}"], writes=[f"awA{par}h{hf}"])
                pk = c.a_pA[:, AB * 512: AB * 512 + TtK]
                pkk = pbanks("apA", AB * 512, TtK)
                for dc in range(NDC):
                    hf = dc // (NDC // 2)
                    P.add("pe", lambda e, dc=dc, wb=wb, pk=pk, par=par: e.matmul(
                        pk, wb[:, dc * 128:(dc + 1) * 128], c.a_xk[dc][:],
                        start=(dc == 0), stop=(dc == NDC - 1)),
                        reads=[f"awA{par}h{hf}", f"axk{dc}"], writes=pkk)
                kf = c.a_kf[(i % 2) * 2 + AB]
                kfk = f"akf{(i%2)*2+AB}"
                P.add("act", lambda e, blk=blk, pk=pk, kf=kf: e.activation(
                    kf[:], pk, AF.Identity, bias=c.s_bk[:, blk:blk + 1]),
                    reads=pkk + ["s_bk"], writes=[kfk])
                kfs.append((kf, kfk))
            emit_rope(c, kfs[0][0], kfs[0][1], kfs[1][0], kfs[1][1], c.a_K[i],
                      [f"K:{2*i}:{kt}", f"K:{2*i+1}:{kt}"], t0, TtK, False,
                      ceb, cek, seb, sek)
        for oc in range(NH):
            par = oc % 2
            wb = c.a_wB[par]
            for hf in range(2):
                P.dma("qs", lambda e, oc=oc, wb=wb, hf=hf: e.dma_start(
                    wb[:, hf * NDC * 64:(hf + 1) * NDC * 64],
                    c.wv_sc[oc * 128:(oc + 1) * 128,
                            hf * NDC * 64:(hf + 1) * NDC * 64]),
                    reads=[f"wv:{oc}"], writes=[f"awB{par}h{hf}"])
            pv = c.a_pB[:, par * 512: par * 512 + TtK]
            pvk = pbanks("apB", par * 512, TtK)
            for dc in range(NDC):
                hf = dc // (NDC // 2)
                P.add("pe", lambda e, dc=dc, oc=oc, wb=wb, pv=pv: e.matmul(
                    pv, wb[:, dc * 128:(dc + 1) * 128], c.a_xk[dc][:],
                    start=(dc == 0), stop=(dc == NDC - 1)),
                    reads=[f"awB{par}h{hf}", f"axk{dc}"], writes=pvk)
            vt = c.a_vt[par]
            P.add("act", lambda e, oc=oc, pv=pv, vt=vt: e.activation(
                vt[:, 0:TtK], pv, AF.Identity, bias=c.s_bv[:, oc:oc + 1]),
                reads=pvk + ["s_bv"], writes=[f"avt{par}"])
            for i in range(TtK // 128):
                P.add("pe", lambda e, i=i, vt=vt: e.transpose(
                    c.a_pT[:, i * 128:(i + 1) * 128], vt[:, i * 128:(i + 1) * 128],
                    c.s_ident_bf[:]),
                    reads=[f"avt{par}", "s_ident_bf"], writes=["apT"])
            for i in range(TtK // 128):
                tc = t0 // 128 + i
                P.add("act", lambda e, i=i, tc=tc, oc=oc: e.activation(
                    c.a_V[:, tc * c.cfg.D + oc * 128: tc * c.cfg.D + oc * 128 + 128],
                    c.a_pT[:, i * 128:(i + 1) * 128], AF.Copy),
                    reads=["apT"], writes=[f"V:{tc}:{oc}"])

    stats(0)
    rowsbc(0)
    if nhalf > 1:
        stats(1)
        rowsbc(1)
    nxt = min(nhalf, 2)
    for kt in range(nkt):
        for s in range(nsub):
            xn(kt * nsub + s)
        hi = min(nhalf, (kt + 1) * nsub + 2)
        pre = nxt < hi
        if pre:
            stats(nxt)
        proj(kt)
        if pre:
            rowsbc(nxt)
            nxt += 1
        while nxt < hi:
            stats(nxt)
            rowsbc(nxt)
            nxt += 1
        emit_k_norms(c, kt)
        if after_tile is not None:
            after_tile(kt)


def emit_k_norms(c, kt):
    P, cfg = c.P, c.cfg
    T, NH, TtK = cfg.T, cfg.NH, cfg.TtK
    t0 = kt * TtK
    for i in range(NH // 2):
        par = i % 2
        ks = c.a_ksq[par % len(c.a_ksq)]
        kkeys = [f"K:{2*i+j}:{kt}" for j in (0, 1)]
        P.add("act", lambda e, i=i, ks=ks, t0=t0: e.activation(
            ks[:, 0:TtK], c.a_K[i][:, t0:t0 + TtK], AF.Square),
            reads=kkeys, writes=["aksq0"])
        for hb2 in (0, 64):
            h = 2 * i + hb2 // 64
            for (o, w) in slices(TtK):
                first = kt == 0 and o == 0
                P.add("pe", lambda e, ks=ks, hb2=hb2, o=o, w=w: e.matmul(
                    c.a_pL[0:1, 0:w], c.ones_col_bf[hb2:hb2 + 64, 0:1],
                    ks[hb2:hb2 + 64, o:o + w], start=True, stop=True),
                    reads=["ones_col_bf", "aksq0"], writes=["apL"])
                P.add("dve", lambda e, w=w: e.tensor_reduce(
                    c.a_us[0:1, 0:1], c.a_pL[0:1, 0:w], AX.X, ALU.max),
                    reads=["apL"], writes=["aus"])
                if first:
                    P.add("dve", lambda e, h=h: e.tensor_copy(
                        c.a_mk2[0:1, h:h + 1], c.a_us[0:1, 0:1]),
                        reads=["aus"], writes=["amk2"])
                else:
                    P.add("dve", lambda e, h=h: e.tensor_max(
                        c.a_mk2[0:1, h:h + 1], c.a_mk2[0:1, h:h + 1],
                        c.a_us[0:1, 0:1]),
                        reads=["aus", "amk2"], writes=["amk2"])


def make_attn_groups(c):
    P, cfg = c.P, c.cfg
    D, T, NDC, NH, QG = cfg.D, cfg.T, cfg.NDC, cfg.NH, cfg.QG
    nq = QG // 128
    pA_names = [("apA", c.a_pA), ("apB", c.a_pB)]
    pending_outproj = [None]

    def outproj(g):
        qt0 = g * QG
        gp = g % 2
        for doc in range(NDC):
            par = doc % 2
            wb = c.a_wB[par]
            wcols = NH * 128
            for hf in range(2):
                P.dma("qs", lambda e, doc=doc, wb=wb, hf=hf, wcols=wcols:
                      e.dma_start(
                    wb[:, hf * wcols // 2:(hf + 1) * wcols // 2],
                    c.wo_sc[doc * 128:(doc + 1) * 128,
                            hf * wcols // 2:(hf + 1) * wcols // 2]),
                    reads=[f"wo:{doc}"], writes=[f"awB{par}h{hf}"])
            po = c.a_pB[:, par * 512: par * 512 + QG]
            pok = pbanks("apB", par * 512, QG)
            for oc in range(NH):
                hf = oc // (NH // 2)
                P.add("pe", lambda e, oc=oc, wb=wb, po=po: e.matmul(
                    po, wb[:, oc * 128:(oc + 1) * 128], c.a_ao[:, oc * QG:(oc + 1) * QG],
                    start=(oc == 0), stop=(oc == NH - 1)),
                    reads=[f"awB{par}h{hf}", f"ao:{oc}"], writes=pok)
            ocs = c.a_oc[par]
            zb = c.a_z[par]
            rqb = c.a_rq[par]
            P.dma("qs", lambda e, doc=doc, qt0=qt0, rqb=rqb: e.dma_start(
                rqb[:], c.R1[doc * 128:(doc + 1) * 128, qt0:qt0 + QG]),
                reads=rkeys("R1", doc, qt0, qt0 + QG), writes=[f"arq{par}"])
            P.add("dve", lambda e, zb=zb, rqb=rqb, gp=gp: e.tensor_sub(
                zb[:], rqb[:], c.a_BmF[gp][:, 0:QG]),
                reads=[f"arq{par}", f"aBmF{gp}"], writes=[f"az{par}"])
            P.add("dve", lambda e, zb=zb, gp=gp: e.tensor_mul(
                zb[:], zb[:], c.a_Br1F[gp][:, 0:QG]),
                reads=[f"az{par}", f"aBr1F{gp}"], writes=[f"az{par}"])
            P.add("dve", lambda e, doc=doc, po=po, ocs=ocs, rqb=rqb: e.scalar_tensor_tensor(
                ocs[:], po, c.s_bo[:, doc:doc + 1], rqb[:], ALU.add, ALU.add),
                reads=pok + ["s_bo", f"arq{par}"], writes=[f"aoc{par}"])
            P.add("dve", lambda e, ocs=ocs, zb=zb: e.tensor_add(ocs[:], ocs[:], zb[:]),
                  reads=[f"aoc{par}", f"az{par}"], writes=[f"aoc{par}"])
            P.dma("act", lambda e, doc=doc, ocs=ocs, qt0=qt0: e.dma_start(
                c.R2[doc * 128:(doc + 1) * 128, qt0:qt0 + QG], ocs[:]),
                reads=[f"aoc{par}"], writes=rkeys("R2", doc, qt0, qt0 + QG))

    def emit_one_group(g):
        qt0 = g * QG
        nt = (qt0 + QG) // 128
        gp = g % 2
        TN = min(256, cfg.TtK)
        js = (qt0 // TN) % 4
        off = qt0 % TN
        so = qt0 % cfg.TtK
        emit_bcast(c, c.a_rowF[0][js], f"arF0_{js}", c.a_BmF[gp], f"aBmF{gp}",
                   QG, f32=True, o0=off, pb=(js % 2) * 32)
        emit_bcast(c, c.a_rowF[1][js], f"arF1_{js}", c.a_Br1F[gp], f"aBr1F{gp}",
                   QG, f32=True, o0=off, pb=(js % 2) * 32)
        if pending_outproj[0] is not None:
            outproj(pending_outproj[0])
        pending_outproj[0] = g
        P.dma("qs", lambda e, qt0=qt0: e.dma_start(
            c.a_ceq[:, 0:QG], c.ce_t[:, qt0:qt0 + QG]), writes=["a_ceq"])
        P.dma("qs", lambda e, qt0=qt0: e.dma_start(
            c.a_seq[:, 0:QG], c.se_t[:, qt0:qt0 + QG]), writes=["a_seq"])
        for i in range(NH // 2):
            qfs = []
            for AB in range(2):
                blk = AB * (NH // 2) + i
                par = (2 * i + AB) % 2
                wb = c.a_wA[par]
                for hf in range(2):
                    P.dma("qs", lambda e, blk=blk, wb=wb, hf=hf: e.dma_start(
                        wb[:, hf * NDC * 64:(hf + 1) * NDC * 64],
                        c.wq_sc[blk * 128:(blk + 1) * 128,
                                hf * NDC * 64:(hf + 1) * NDC * 64]),
                        reads=[f"wq:{blk}"], writes=[f"awA{par}h{hf}"])
                pq = c.a_pA[:, AB * 512: AB * 512 + QG]
                pqk = pbanks("apA", AB * 512, QG)
                for dc in range(NDC):
                    hf = dc // (NDC // 2)
                    P.add("pe", lambda e, dc=dc, wb=wb, pq=pq, par=par, so=so:
                          e.matmul(
                        pq, wb[:, dc * 128:(dc + 1) * 128],
                        c.a_xk[dc][:, so:so + QG],
                        start=(dc == 0), stop=(dc == NDC - 1)),
                        reads=[f"awA{par}h{hf}", f"axk{dc}"], writes=pqk)
                qf = c.a_qf[(i % 2) * 2 + AB]
                qfk = f"aqf{(i%2)*2+AB}"
                P.add("act", lambda e, blk=blk, pq=pq, qf=qf: e.activation(
                    qf[:], pq, AF.Identity, bias=c.s_bq[:, blk:blk + 1]),
                    reads=pqk + ["s_bq"], writes=[qfk])
                qfs.append((qf, qfk))
            emit_rope(c, qfs[0][0], qfs[0][1], qfs[1][0], qfs[1][1], c.a_q[i],
                      [f"q:{2*i}", f"q:{2*i+1}"], qt0, QG, True,
                      c.a_ceq, "a_ceq", c.a_seq, "a_seq")
        for i in range(NH // 2):
            r = i % 2
            P.add("act", lambda e, i=i, r=r: e.activation(
                c.a_rsq[r][:, 0:QG], c.a_q[i][:, 0:QG], AF.Square),
                reads=[f"q:{2*i}", f"q:{2*i+1}"], writes=[f"arsq{r}"])
            pb_, pbk_ = (c.a_pL, "apL") if i % 2 == 0 else (c.a_pPV, "apPV")
            for hb2 in (0, 64):
                h = 2 * i + hb2 // 64
                P.add("pe", lambda e, r=r, hb2=hb2, pb_=pb_: e.matmul(
                    pb_[0:1, 0:QG], c.ones_col_bf[hb2:hb2 + 64, 0:1],
                    c.a_rsq[r][hb2:hb2 + 64, 0:QG], start=True, stop=True),
                    reads=["ones_col_bf", f"arsq{r}"], writes=[pbk_])
                P.add("dve", lambda e, h=h, pb_=pb_: e.tensor_reduce(
                    c.a_qm2[0:1, h:h + 1], pb_[0:1, 0:QG], AX.X, ALU.max),
                    reads=[pbk_], writes=["aqm2"])
        ubr = c.a_ubr[g % 2]
        ubk = f"aubr{g%2}"
        P.add("dve", lambda e, ubr=ubr: e.tensor_mul(
            ubr[0:1, 0:NH], c.a_qm2[0:1, 0:NH], c.a_mk2[0:1, 0:NH]),
            reads=["aqm2", "amk2"], writes=[ubk])
        P.add("act", lambda e, ubr=ubr: e.activation(ubr[0:1, 0:NH], ubr[0:1, 0:NH],
                                                     AF.Ln),
              reads=[ubk], writes=[ubk])
        P.add("act", lambda e, ubr=ubr: e.activation(ubr[0:1, 0:NH], ubr[0:1, 0:NH],
                                                     AF.Exp, scale=0.5),
              reads=[ubk], writes=[ubk])
        P.add("dve", lambda e, ubr=ubr: e.tensor_scalar(
            ubr[0:1, 0:NH], ubr[0:1, 0:NH], -INV_SQRT_HD, UC, ALU.mult, ALU.add),
            reads=[ubk], writes=[ubk])
        P.add("pe", lambda e, ubr=ubr: e.matmul(
            c.a_pC[:, 0:NH], c.ones_row_f[0:1, :], ubr[0:1, 0:NH],
            start=True, stop=True),
            reads=["ones_row_f", ubk], writes=["apC"])
        P.add("act", lambda e: e.activation(c.a_ubA[:, 0:NH], c.a_pC[:, 0:NH],
                                            AF.Copy),
              reads=["apC"], writes=["aubA"])
        rotbox = [0]

        def front(h):
            hb = (h % 2) * 64
            ub = c.a_ubA[:, h:h + 1]
            ubik = "aubA"
            PT = c.a_PT[h % 2]
            ptk = f"PT{h%2}"
            nbat = QG * 4 // 256
            for tb0 in range(0, nt, nbat):
                tbw = min(nbat, nt - tb0)
                rot = rotbox[0]
                pname, pten = pA_names[rot % 2]
                rotbox[0] += 1
                for i in range(tbw):
                    tc = tb0 + i
                    qtc0 = qt0 // 128
                    if tc < qtc0:
                        mconst, mkey = None, None
                    elif tc == qtc0 or nq == 1:
                        mconst, mkey = c.s_mc1, "s_mc1"
                    else:
                        mconst, mkey = c.s_mc2, "s_mc2"
                    P.add("pe", lambda e, h=h, hb=hb, tc=tc, i=i, pten=pten,
                          st=(mconst is None): e.matmul(
                        pten[:, i * QG:(i + 1) * QG],
                        c.a_K[h // 2][hb:hb + 64, tc * 128:(tc + 1) * 128],
                        c.a_q[h // 2][hb:hb + 64, :],
                        start=True, stop=st),
                        reads=[f"K:{h}:{tc*128//cfg.TtK}", f"q:{h}"],
                        writes=pbanks(pname, i * QG, QG))
                    if mconst is not None:
                        P.add("pe", lambda e, i=i, mconst=mconst, pten=pten: e.matmul(
                            pten[:, i * QG:(i + 1) * QG],
                            c.s_ident_bf[:], mconst[:], start=False, stop=True),
                            reads=["s_ident_bf", mkey],
                            writes=pbanks(pname, i * QG, QG))
                P.add("act", lambda e, tb0=tb0, tbw=tbw, pten=pten, PT=PT,
                      ub=ub: e.activation(
                    PT[:, tb0 * QG:(tb0 + tbw) * QG], pten[:, 0:tbw * QG],
                    AF.Exp, scale=INV_SQRT_HD, bias=ub[:, 0:1]),
                    reads=pbanks(pname, 0, tbw * QG) + [ubik],
                    writes=[f"{ptk}:{tb0 + i}" for i in range(tbw)])
        def pv(h):
            PT = c.a_PT[h % 2]
            ptk = f"PT{h%2}"
            for tc in range(nt):
                P.add("pe", lambda e, h=h, tc=tc, PT=PT, nt=nt: e.matmul(
                    c.a_pPV[:, 0:QG],
                    c.a_V[:, tc * D + h * 128: tc * D + h * 128 + 128],
                    PT[:, tc * QG:(tc + 1) * QG],
                    start=(tc == 0), stop=(tc == nt - 1)),
                    reads=[f"V:{tc}:{h}", f"{ptk}:{tc}"], writes=["apPV"])
            for tc in range(nt):
                P.add("pe", lambda e, tc=tc, PT=PT, nt=nt: e.matmul(
                    c.a_pL[0:1, 0:QG], c.ones_col_bf[:], PT[:, tc * QG:(tc + 1) * QG],
                    start=(tc == 0), stop=(tc == nt - 1)),
                    reads=["ones_col_bf", f"{ptk}:{tc}"], writes=["apL"])
            P.add("act", lambda e, h=h: e.activation(
                c.a_ao[:, h * QG:(h + 1) * QG], c.a_pPV[:, 0:QG], AF.Copy),
                reads=["apPV"], writes=[f"ao:{h}"])
            P.add("dve", lambda e: e.reciprocal(c.a_linv[0:1, 0:QG], c.a_pL[0:1, 0:QG]),
                  reads=["apL"], writes=["alinv"])
            P.add("dve", lambda e, h=h: e.tensor_copy(
                c.a_linv_bf[h % 2][0:1, 0:QG], c.a_linv[0:1, 0:QG]),
                reads=["alinv"], writes=[f"alinv_bf{h%2}"])

        def fin(h):
            P.add("pe", lambda e, h=h: e.matmul(
                c.a_pC[:, 0:QG], c.ones_row_bf[:], c.a_linv_bf[h % 2][0:1, 0:QG],
                start=True, stop=True),
                reads=["ones_row_bf", f"alinv_bf{h%2}"], writes=["apC"])
            P.add("act", lambda e: e.activation(c.a_Blinv[:, 0:QG], c.a_pC[:, 0:QG],
                                                AF.Copy),
                  reads=["apC"], writes=["aBlinv"])
            P.add("dve", lambda e, h=h: e.tensor_mul(
                c.a_ao[:, h * QG:(h + 1) * QG], c.a_ao[:, h * QG:(h + 1) * QG],
                c.a_Blinv[:, 0:QG]),
                reads=[f"ao:{h}", "aBlinv"], writes=[f"ao:{h}"])

        for h in range(NH):
            front(h)
            if h >= 1:
                pv(h - 1)
            if h >= 2:
                fin(h - 2)
        pv(NH - 1)
        fin(NH - 2)
        fin(NH - 1)

    def finish():
        outproj(pending_outproj[0])

    return emit_one_group, finish



def prep_inputs(cfg, x_core, rms_w, wq, bq, wk, bk, wv, bv, wo, bo,
                wg, sg, wu, su, wd, sd):
    D, T, HID = cfg.D, cfg.T, cfg.HID
    NDC, NH, NHC = cfg.NDC, cfg.NH, cfg.NHC
    permA = np.concatenate([h * 128 + np.arange(0, 128, 2) for h in range(NH)])
    permB = np.concatenate([h * 128 + np.arange(1, 128, 2) for h in range(NH)])
    perm = np.concatenate([permA, permB])

    def tile_in_out(w):
        nin, nout = w.shape[0] // 128, w.shape[1] // 128
        return np.ascontiguousarray(
            w.reshape(nin, 128, nout, 128).transpose(2, 1, 0, 3).reshape(nout * 128, nin * 128))

    def tile_quant(w):
        nout, nin = w.shape[0] // 128, w.shape[1] // 128
        return np.ascontiguousarray(
            w.reshape(nout, 128, nin, 128).transpose(0, 3, 2, 1).reshape(nout * 128, nin * 128))

    col = lambda v: np.ascontiguousarray(v.reshape(-1, 128).T.astype(np.float32))

    j = np.arange(64)
    freqs = cfg.theta ** (j.astype(np.float64) * (-2.0 / 128.0))
    pos = np.arange(T, dtype=np.float64)
    ang = pos[None, :] * freqs[:, None]
    cosa, sina = np.cos(ang), np.sin(ang)
    even = (j % 2 == 0)[:, None]
    ce = np.where(even, cosa, sina).astype(np.float32)
    se = np.where(even, -sina, cosa).astype(np.float32)
    ce = np.concatenate([ce, ce], axis=0)
    se = np.concatenate([se, se], axis=0)

    ii = np.arange(128)
    mask_s = np.where(ii[None, :] <= ii[:, None], 0.0, NEG).astype(np.float32)
    mask_t = np.where(ii[:, None] <= ii[None, :], 0.0, NEG).astype(np.float32)
    mask_fl = np.full((128, 128), NEG, dtype=np.float32)
    zeros = np.zeros((128, 128), dtype=np.float32)
    maskw_s = np.concatenate([zeros] * 4 + [mask_s], axis=1)
    nq = cfg.QG // 128
    if nq == 1:
        mask_c1 = mask_t
        mask_c2 = mask_t
    else:
        mask_c1 = np.concatenate([mask_t, zeros], axis=1)
        mask_c2 = np.concatenate([mask_fl, mask_t], axis=1)
    ident = np.eye(128, dtype=np.float32)

    return {
        "xT": np.ascontiguousarray(x_core.T).astype(np.float32),
        "wq_t": tile_in_out(wq[:, perm]),
        "wk_t": tile_in_out(wk[:, perm]),
        "wv_t": tile_in_out(wv),
        "wo_t": tile_in_out(wo),
        "bq_col": col(bq[perm]), "bk_col": col(bk[perm]),
        "bv_col": col(bv), "bo_col": col(bo), "gam_col": col(rms_w),
        "wg_t": tile_quant(wg), "wu_t": tile_quant(wu), "wd_t": tile_quant(wd),
        "sg_col": col(sg), "su_col": col(su), "sd_col": col(sd),
        "ce_t": ce.astype(ml_dtypes.bfloat16), "se_t": se.astype(ml_dtypes.bfloat16),
        "maskw_s": maskw_s.astype(ml_dtypes.bfloat16),
        "mask_c1": mask_c1.astype(ml_dtypes.bfloat16),
        "mask_c2": mask_c2.astype(ml_dtypes.bfloat16),
        "ident_f": ident, "ident_bf": ident.astype(ml_dtypes.bfloat16),
    }


def kernel(L0_attn_x, L0_attn_rms_w, L0_attn_wq, L0_attn_bq, L0_attn_wk, L0_attn_bk,
           L0_attn_wv, L0_attn_bv, L0_attn_wo, L0_attn_bo,
           L0_ffn_wq_gate, L0_ffn_s_gate, L0_ffn_wq_up, L0_ffn_s_up,
           L0_ffn_wq_down, L0_ffn_s_down):
    cfg = FULL
    x = np.asarray(L0_attn_x)
    n_cores = x.shape[0]
    nc, c = build_nc(cfg)
    common = None
    in_maps = []
    for b in range(n_cores):
        m = prep_inputs(cfg, x[b],
                        np.asarray(L0_attn_rms_w), np.asarray(L0_attn_wq),
                        np.asarray(L0_attn_bq), np.asarray(L0_attn_wk),
                        np.asarray(L0_attn_bk), np.asarray(L0_attn_wv),
                        np.asarray(L0_attn_bv), np.asarray(L0_attn_wo),
                        np.asarray(L0_attn_bo), np.asarray(L0_ffn_wq_gate),
                        np.asarray(L0_ffn_s_gate), np.asarray(L0_ffn_wq_up),
                        np.asarray(L0_ffn_s_up), np.asarray(L0_ffn_wq_down),
                        np.asarray(L0_ffn_s_down)) if common is None else dict(common)
        if common is None:
            common = {k: v for k, v in m.items() if k != "xT"}
        m["xT"] = np.ascontiguousarray(x[b].T).astype(np.float32)
        in_maps.append(m)
    res = run_bass_kernel_spmd(nc, in_maps, core_ids=list(range(n_cores)))
    outs = [np.ascontiguousarray(np.asarray(r["out"]).T) for r in res.results]
    return np.stack(outs, axis=0)
